# version 06
# baseline (speedup 1.0000x reference)
import numpy as np
import ml_dtypes
import concourse.bass as bass
import concourse.mybir as mybir
from concourse.bass_utils import run_bass_kernel_spmd

F32 = mybir.dt.float32
BF16 = mybir.dt.bfloat16
U32 = mybir.dt.uint32
AF = mybir.ActivationFunctionType
ALU = mybir.AluOpType
AX = mybir.AxisListType

D = 1024
ND = D // 128
NONCE_VAL = 0x5EED0001

BF = ml_dtypes.bfloat16



class Sem:
    __slots__ = ("h", "n", "name")

    def __init__(self, h, name):
        self.h = h
        self.n = 0
        self.name = name

    def bump(self, k=1):
        self.n += k
        return self.n


class Ctx:

    def __init__(self, nc, block, T):
        self.nc = nc
        self.block = block
        self.T = T
        self.NT = T // 512
        self.TT = T // 128
        self.sb = {}
        self.ps = []
        self.sems = {}
        self.last = {}
        self.last_is_dma = {}
        self.last_update = {}
        self.consts = {}
        self.nexch = 0
        self.uid = 0
        self.exp_done = {}
        self.nrm_free = {}
        self.obank_free = {}
        self.lnr_free = {}
        self.dram = {}
        self.poll_idx = 0
        self.dma_sems = set()
        self.psfree = {}

    def sem(self, name):
        if name not in self.sems:
            h = self.nc._stack.enter_context(self.nc.semaphore(name))
            self.sems[name] = Sem(h, name)
        return self.sems[name]

    def sec(self, eng):
        return getattr(self.block, eng)


ENGINES = ("tensor", "vector", "scalar", "gpsimd", "sync")


def fin(C, eng, inst):
    C.last[eng] = inst
    return inst


def barrier(C, engines=ENGINES):
    marks = {}
    for eng in engines:
        upd = C.last_update.get(eng)
        if upd is not None:
            marks[eng] = upd
        else:
            s = C.sem("bar_" + eng)
            inst = C.last.get(eng)
            if inst is not None and not C.last_is_dma.get(eng):
                inst.then_inc(s.h, 1)
            else:
                @C.sec(eng)
                def _(e, s=s):
                    e.sem_inc(s.h, 1)
            marks[eng] = (s, s.bump())
        C.last[eng] = None
        C.last_update[eng] = None
        C.last_is_dma[eng] = False
    dma_waits = all_dma_waits(C)
    for eng in engines:
        @C.sec(eng)
        def _(e, eng=eng):
            for o in engines:
                if o != eng:
                    s, v = marks[o]
                    e.wait_ge(s.h, v)
            for s, v in dma_waits:
                e.wait_ge(s.h, v)



def _bias_cols(v):
    n = v.shape[0]
    assert n % 128 == 0 or n <= 128
    if n <= 128:
        out = np.zeros((128, 1), np.float32)
        out[:n, 0] = v
        return out
    return np.ascontiguousarray(v.reshape(n // 128, 128).T).astype(np.float32)


def host_prepare(inputs, T=2048):
    B, L, _ = inputs["L0_attn_x"].shape
    x = np.asarray(inputs["L0_attn_x"], np.float32)
    per_core = []
    g = {}

    def T_(a):
        return np.ascontiguousarray(np.asarray(a, np.float32).T)

    def C_(a):
        return np.ascontiguousarray(np.asarray(a, np.float32))

    g["l0_wqkv"] = C_(inputs["L0_attn_W_qkv"])
    g["l0_wout"] = C_(inputs["L0_attn_W_out"])
    g["l0_gate"] = T_(inputs["L0_ffn_gate_w"])
    g["l0_wup"] = np.ascontiguousarray(
        np.asarray(inputs["L0_ffn_w_up"], np.float32).transpose(0, 2, 1))
    g["l0_wdn"] = np.ascontiguousarray(
        np.asarray(inputs["L0_ffn_w_down"], np.float32).transpose(0, 2, 1))
    g["l0_sup"] = T_(inputs["L0_ffn_shared_w_up"])
    g["l0_sdn"] = T_(inputs["L0_ffn_shared_w_down"])
    g["l1_wqkv"] = C_(inputs["L1_attn_Wqkv"])
    g["l1_wout"] = C_(inputs["L1_attn_Wout"])
    g["l1_gate"] = C_(inputs["L1_ffn_gate_w"])
    g["l1_ew1"] = C_(inputs["L1_ffn_ew1"])
    g["l1_ew2"] = C_(inputs["L1_ffn_ew2"])
    g["l1_ew3"] = C_(inputs["L1_ffn_ew3"])
    g["l1_sw1"] = C_(inputs["L1_ffn_sw1"])
    g["l1_sw2"] = C_(inputs["L1_ffn_sw2"])
    g["l1_sw3"] = C_(inputs["L1_ffn_sw3"])
    g["l2_wqv"] = T_(inputs["L2_attn_W_qkv_v"])
    g["l2_wqu"] = T_(inputs["L2_attn_W_qkv_u"])
    g["l2_wov"] = T_(inputs["L2_attn_W_o_v"])
    g["l2_wou"] = T_(inputs["L2_attn_W_o_u"])
    g["l2_wg"] = C_(inputs["L2_ffn_wg"])
    w1 = np.asarray(inputs["L2_ffn_w1"], np.float32)
    g["l2_w1a"] = np.ascontiguousarray(w1[:, :, 0::2])
    g["l2_w1b"] = np.ascontiguousarray(w1[:, :, 1::2])
    g["l2_w2"] = C_(inputs["L2_ffn_w2"])
    g["l2_b2m"] = C_(inputs["L2_ffn_b2"])
    g["l3_wxz"] = C_(inputs["L3_attn_W_xz"])
    g["l3_wdt"] = C_(inputs["L3_attn_W_dt"])
    g["l3_wb"] = C_(inputs["L3_attn_W_B"])
    g["l3_wc"] = C_(inputs["L3_attn_W_C"])
    g["l3_wout"] = C_(inputs["L3_attn_W_out"])
    g["l3_alog"] = np.ascontiguousarray(
        np.asarray(inputs["L3_attn_A_log"], np.float32).reshape(8, 128, 16)
        .transpose(1, 0, 2).reshape(128, 128))
    g["l3_aw1"] = C_(inputs["L3_ffn_aw1"])
    g["l3_aw2"] = C_(inputs["L3_ffn_aw2"])
    g["l3_gate"] = C_(inputs["L3_ffn_gate_w"])
    g["l3_ew1"] = C_(inputs["L3_ffn_ew1"])
    g["l3_ew2"] = C_(inputs["L3_ffn_ew2"])
    g["l3_gens"] = C_(inputs["L3_ffn_gens"])
    g["l3_gcw"] = C_(inputs["L3_ffn_gc_w"])
    g["l3_eb2m"] = C_(inputs["L3_ffn_eb2"])

    bias_list = [
        ("l0_bqkv", inputs["L0_attn_b_qkv"]),
        ("l0_bout", inputs["L0_attn_b_out"]),
        ("l0_ebias", inputs["L0_ffn_expert_bias"]),
        ("l1_gamma", inputs["L1_attn_gamma"]),
        ("l1_beta", inputs["L1_attn_beta"]),
        ("l2_nscale", inputs["L2_ffn_norm_scale"]),
        ("l2_bg", inputs["L2_ffn_bg"]),
        ("l2_b1a", np.asarray(inputs["L2_ffn_b1"], np.float32)[:, 0::2].reshape(-1)),
        ("l2_b1b", np.asarray(inputs["L2_ffn_b1"], np.float32)[:, 1::2].reshape(-1)),
        ("l2_b2", np.asarray(inputs["L2_ffn_b2"], np.float32).reshape(-1)),
        ("l3_bdt", inputs["L3_attn_b_dt"]),
        ("l3_dskip", inputs["L3_attn_D_skip"]),
        ("l3_vth", np.maximum(np.asarray(inputs["L3_attn_v_th"], np.float32), 0.1)),
        ("l3_lng", inputs["L3_ffn_ln_g"]),
        ("l3_lnb", inputs["L3_ffn_ln_b"]),
        ("l3_ab1", inputs["L3_ffn_ab1"]),
        ("l3_ab2", inputs["L3_ffn_ab2"]),
        ("l3_gateb", inputs["L3_ffn_gate_b"]),
        ("l3_eb1", np.asarray(inputs["L3_ffn_eb1"], np.float32).reshape(-1)),
        ("l3_eb2", np.asarray(inputs["L3_ffn_eb2"], np.float32).reshape(-1)),
        ("l3_gcb", np.full((1,), float(np.asarray(inputs["L3_ffn_gc_b"]).reshape(-1)[0]), np.float32)),
        ("l3_gcw", np.asarray(inputs["L3_ffn_gc_w"], np.float32).reshape(-1)),
    ]
    bias_list = [(n, (inputs["L3_attn_v_th"] if n == "l3_vth" else v)) for n, v in bias_list]

    bias_reg = {}
    blocks = []
    off = 0
    for name, v in bias_list:
        b = _bias_cols(np.asarray(v, np.float32).reshape(-1))
        bias_reg[name] = (off, b.shape[1])
        blocks.append(b)
        off += b.shape[1]
    g["biases"] = np.ascontiguousarray(np.concatenate(blocks, axis=1))

    g["identity"] = np.eye(128, dtype=np.float32)
    g["ones_row"] = np.ones((1, 128), np.float32)
    oh = np.zeros((16, 64, 128), np.float32)
    for i in range(16):
        oh[i, i, :] = 1.0
        oh[i, 32 + i, :] = 1.0
    g["onehot32"] = np.ascontiguousarray(oh.transpose(1, 0, 2).reshape(64, 16 * 128))
    g["l0_ebias_rep"] = np.ascontiguousarray(
        np.repeat(np.asarray(inputs["L0_ffn_expert_bias"], np.float32)[None, :], 128, axis=0))
    g["nonce"] = np.array([[NONCE_VAL]], np.uint32)

    for c in range(8):
        b, h = c // 2, c % 2
        m = dict(g)
        m["xT"] = np.ascontiguousarray(x[b, h * T:(h + 1) * T, :].T)
        m["slot"] = np.array([[h]], np.uint32)
        m["oddmask"] = np.array([[float(h)]], np.float32)
        m["oddcol"] = np.full((128, 1), float(h), np.float32)
        per_core.append(m)
    return per_core, bias_reg



def dma(C, eng, sname, fn, waits=()):
    s = C.sem(sname)
    t = op(C, eng, fn, waits=waits, inc=s, inck=16)
    C.dma_sems.add(sname)
    return (s, t)


def all_dma_waits(C):
    return [(C.sems[n], C.sems[n].n) for n in sorted(C.dma_sems)]


def op(C, eng, fn, waits=(), inc=None, inck=1):
    @C.sec(eng)
    def _(e):
        for s, v in waits:
            e.wait_ge(s.h, v)
        inst = fn(e)
        if inst is not None:
            if inc is not None:
                inst.then_inc(inc.h, inck)
            C.last[eng] = inst
            C.last_is_dma[eng] = (inck == 16)
            C.last_update[eng] = (inc, inc.n + inck) if inc is not None else None
    if inc is not None:
        return inc.bump(inck)
    return None


def mmgrp(C, psum_ap, pairs, waits=(), per_mm_waits=None, inc=None,
          start_first=True, stop_last=True, bank_idx=None):
    gate = C.psfree.pop(bank_idx, None) if (bank_idx is not None and start_first) else None

    @C.sec("tensor")
    def _(e):
        if gate is not None:
            e.wait_ge(gate[0].h, gate[1])
        n = len(pairs)
        for i, (l, r) in enumerate(pairs):
            ws = per_mm_waits[i] if per_mm_waits is not None else (waits if i == 0 else ())
            for s, v in ws:
                e.wait_ge(s.h, v)
            inst = e.matmul(psum_ap, l, r, start=(i == 0 and start_first),
                            stop=(i == n - 1 and stop_last))
        if inc is not None:
            inst.then_inc(inc.h, 1)
        C.last["tensor"] = inst
        C.last_is_dma["tensor"] = False
        C.last_update["tensor"] = (inc, inc.n + 1) if inc is not None else None
    if inc is not None:
        return inc.bump(1)
    return None




class WRing:

    def __init__(self, C, nslots=8):
        self.C = C
        self.n = nslots
        self.next = 0
        self.freed_at = [None] * nslots

    def load(self, dram_ap, ncols):
        C = self.C
        i = self.next
        self.next = (self.next + 1) % self.n
        gate = self.freed_at[i]
        tgt = dma(C, "gpsimd", f"dw{i}", lambda e: e.dma_start(
            out=C.sb["w16"][:, i * 512:i * 512 + ncols], in_=dram_ap),
            waits=([gate] if gate is not None else []))
        return C.sb["w16"][:, i * 512:i * 512 + ncols], tgt, i

    def load_shaped(self, dram_ap_3d, ncols, a, f):
        C = self.C
        i = self.next
        self.next = (self.next + 1) % self.n
        gate = self.freed_at[i]
        out3 = C.sb["w16"][:, i * 512:i * 512 + ncols].rearrange(
            "p (a f) -> p a f", a=a, f=f)
        tgt = dma(C, "gpsimd", f"dw{i}", lambda e: e.dma_start(
            out=out3, in_=dram_ap_3d),
            waits=([gate] if gate is not None else []))
        return C.sb["w16"][:, i * 512:i * 512 + ncols], tgt, i

    def consumed(self, i, sem, val):
        self.freed_at[i] = (sem, val)


def alloc_buffers(C, stack):
    nc, T = C.nc, C.T
    sb = C.sb
    ent = stack.enter_context

    def S(name, shape, dt):
        sb[name] = ent(nc.sbuf_tensor(name, shape, dt))
        return sb[name]

    S("xn16", [128, ND * T], BF16)
    S("f32a", [128, ND * T], F32)
    S("b16a", [128, ND * T], BF16)
    S("st32", [128, 1024], F32)
    S("w16", [128, 4096], BF16)
    S("kv16", [128, 3328], BF16)
    S("tok32", [128, 640], F32)
    S("rows64", [64, 4096], F32)
    S("oh32", [64, 16 * 128], F32)
    S("cst32", [128, 264], F32)
    S("bias32", [128, 704], F32)
    S("sc16", [128, 1072], BF16)
    S("qkstg", [128, 1024], BF16)
    S("beff", [128, 8], F32)
    S("ebias", [128, 8], F32)
    S("bv16", [128, 8], BF16)
    S("ones16", [128, 8], BF16)
    S("aux2", [64, 2048], F32)
    S("fc32", [128, 32], F32)
    S("fc16", [128, 32], BF16)
    S("u32s", [1, 16], U32)
    S("m32", [1, 16], F32)
    S("sc32", [128, 1024], F32)
    S("acol", [128, 128], F32)
    S("moe16", [128, 5632], BF16)
    S("gw16", [128, 512], BF16)
    S("endst", [128, 128], F32)
    S("endst2", [128, 128], F32)
    S("idn16", [128, 128], BF16)
    S("m32c", [128, 1], F32)
    C.sb["sel16"] = C.sb["qkstg"]
    C.sb["scan32"] = C.sb["sc32"]
    C.sb["qblk"] = C.sb["qkstg"]
    C.sb["kvacc"] = C.sb["aux2"]
    C.sb["lnrow"] = C.sb["st32"]
    C.ps = [ent(nc.psum_tensor(f"ps{i}", [128, 512], F32)) for i in range(8)]
    C.ring = WRing(C)


def IDN(C):
    return C.sb["cst32"][:, 0:128]


def ONESC(C, p, f, base=0):
    return C.sb["cst32"][base:base + p, 128:128 + f]


def onehot_lhsT(C, i, rows, base=0):
    assert i < 16
    return C.sb["oh32"][base:base + rows, i * 128:(i + 1) * 128]


def bias_col(C, name, i=0):
    off, n = C.bias_reg[name]
    assert i < n, (name, i, n)
    return C.sb["bias32"][:, off + i:off + i + 1]


def FM(C, arena, dc, lo, n):
    return C.sb[arena][:, dc * C.T + lo:dc * C.T + lo + n]


FCONSTS = [0.0, 1e-5, 1e-6, -20.0, -3.4657359027997265, 1.0, -1.0, 0.5, -0.5, 7.0, -7.0,
           1.702, 0.1, 2.0, 10.0, 0.01, 0.125, 1.0 / 1.702, 0.58775604795]


def load_constants(C, aps):
    sb = C.sb
    sv = C.sem("s_v")
    for i, v in enumerate(FCONSTS):
        op(C, "vector", lambda e, i=i, v=v: e.memset(sb["fc32"][:, i:i + 1], v), inc=sv)
        op(C, "vector", lambda e, i=i, v=v: e.memset(sb["fc16"][:, i:i + 1], v), inc=sv)
        C.nc.const_aps.register(F32, v, sb["fc32"][:, i:i + 1])
        C.nc.const_aps.register(BF16, v, sb["fc16"][:, i:i + 1])
    for out, key in [
        (sb["bias32"][:, 0:C.nbias], "biases"),
        (sb["cst32"][:, 0:128], "identity"),
        (sb["oh32"][:, :], "onehot32"),
        (sb["u32s"][0:1, 0:1], "nonce"),
        (sb["u32s"][0:1, 1:2], "slot"),
        (sb["m32"][0:1, 0:1], "oddmask"),
        (sb["sc32"][:, 0:128], "l3_alog"),
    ]:
        if key not in aps:
            continue
        dcst = dma(C, "sync", "dcst", lambda e, o=out, k=key: e.dma_start(
            out=o, in_=aps[k][:, :]))


    @C.sec("gpsimd")
    def _(e):
        r1 = e.alloc_register()
        e.reg_mov(r1, e.partition_id())
        e.reg_alu(r1, r1, 2, ALU.mod)
        r2 = e.alloc_register()
        e.reg_mov(r2, r1)
        e.reg_alu(r2, r2, C.T, ALU.mult)
        r3 = e.alloc_register()
        e.reg_mov(r3, r1)
        e.reg_alu(r3, r3, 1, ALU.bitwise_xor)
        r4 = e.alloc_register()
        e.reg_mov(r4, r3)
        e.reg_alu(r4, r4, C.T, ALU.mult)
        C.snap_half = e.snap(r1, min_val=0, max_val=1)
        C.snap_halfT = e.snap(r2, min_val=0, max_val=C.T)
        C.snap_ohalf = e.snap(r3, min_val=0, max_val=1)
        C.snap_ohalfT = e.snap(r4, min_val=0, max_val=C.T)
    op(C, "vector", lambda e: e.memset(sb["cst32"][:, 128:264], 1.0), inc=C.sem("s_v"))
    op(C, "vector", lambda e: e.memset(sb["sc16"][:, :], 1.0), inc=C.sem("s_v"))
    op(C, "vector", lambda e: e.memset(sb["rows64"][0:16, :], 0.0), inc=C.sem("s_v"))
    op(C, "vector", lambda e: e.memset(sb["tok32"][:, :], 0.0), inc=C.sem("s_v"))
    op(C, "vector", lambda e: e.memset(sb["ones16"][:, :], 1.0), inc=C.sem("s_v"))
    if "l3_alog" in aps:
        ss = op(C, "scalar", lambda e: e.activation(sb["acol"][:, :], sb["sc32"][:, 0:128],
                                                    AF.Exp), waits=[dcst, (sv, sv.n)],
                inc=C.sem("s_s"))
        op(C, "vector", lambda e: e.tensor_single_scalar(sb["acol"][:, :], sb["acol"][:, :],
                                                         -1.0, ALU.mult),
           waits=[(C.sems["s_s"], ss)], inc=C.sem("s_v"))
    barrier(C)



def emit_stats_rows(C, arena, is16=False):
    T, NT = C.T, C.NT
    sb = C.sb
    st, sv, ss, stq = C.sem("s_t"), C.sem("s_v"), C.sem("s_s"), C.sem("stq")
    R = sb["rows64"]
    ones_mean = C.sb["ones16"][:, 0:1] if is16 else ONESC(C, 128, 1)
    sq_done = []
    stq0 = stq.n
    for i in range(NT * ND):
        tc, dc = divmod(i, ND)
        w = [(stq, stq0 + i - 1)] if i >= 2 else []
        sq_done.append(op(C, "scalar", lambda e, dc=dc, tc=tc, k=i % 2: e.activation(
            sb["sc32"][:, k * 512:(k + 1) * 512], FM(C, arena, dc, tc * 512, 512),
            AF.Square), waits=w, inc=ss))
    for tc in range(NT):
        bank, bank2 = C.ps[tc % 2], C.ps[2 + tc % 2]
        g1 = mmgrp(C, bank[0:1, 0:512],
                   [(ones_mean, FM(C, arena, dc, tc * 512, 512)) for dc in range(ND)],
                   inc=st, bank_idx=tc % 2)
        gate2 = C.psfree.pop(2 + tc % 2, None)

        @C.sec("tensor")
        def _(e, tc=tc, bank2=bank2, gate2=gate2):
            if gate2 is not None:
                e.wait_ge(gate2[0].h, gate2[1])
            for dc in range(ND):
                i = tc * ND + dc
                e.wait_ge(ss.h, sq_done[i])
                inst = e.matmul(bank2[0:1, 0:512], ONESC(C, 128, 1),
                                sb["sc32"][:, (i % 2) * 512:((i % 2) + 1) * 512],
                                start=(dc == 0), stop=(dc == ND - 1))
                inst.then_inc(stq.h, 1)
            C.last["tensor"] = inst
            C.last_is_dma["tensor"] = False
            C.last_update["tensor"] = (stq, stq.n + ND)
        stq.bump(ND)
        g2 = stq.n
        mv = op(C, "vector", lambda e, tc=tc, bank=bank: e.tensor_single_scalar(
            R[32:33, tc * 512:(tc + 1) * 512], bank[0:1, 0:512],
            1.0 / D, ALU.mult), waits=[(st, g1)], inc=sv)
        C.psfree[tc % 2] = (sv, mv)
        sqv = op(C, "vector", lambda e, tc=tc, bank2=bank2: e.tensor_single_scalar(
            R[32:33, T + tc * 512:T + (tc + 1) * 512], bank2[0:1, 0:512],
            1.0 / D, ALU.mult), waits=[(stq, g2)], inc=sv)
        C.psfree[2 + tc % 2] = (sv, sqv)
    prevh = (sv, sv.n)
    for half in range(0, T, 1024):
        hw = min(1024, T - half)
        v1 = op(C, "vector", lambda e, half=half, hw=hw: e.tensor_tensor(
            sb["st32"][32:33, 0:hw], R[32:33, half:half + hw], R[32:33, half:half + hw],
            ALU.mult), waits=[prevh, (stq, stq.n)], inc=sv)
        v2 = op(C, "vector", lambda e, half=half, hw=hw: e.tensor_tensor(
            R[32:33, T + half:T + half + hw], R[32:33, T + half:T + half + hw],
            sb["st32"][32:33, 0:hw], ALU.subtract), waits=[(sv, v1)], inc=sv)
        prevh = (sv, v2)
    v2 = prevh[1]
    s1 = op(C, "scalar", lambda e: e.activation(
        R[32:33, T:2 * T], R[32:33, T:2 * T], AF.Ln, bias=1e-5), waits=[(sv, v2)], inc=ss)
    s2 = op(C, "scalar", lambda e: e.activation(
        R[32:33, T:2 * T], R[32:33, T:2 * T], AF.Exp, scale=-0.5), waits=[(ss, s1)], inc=ss)
    return (ss, s2)


def emit_ln_apply(C, arena, rows_done, dst="xn16", gamma=None, beta=None):
    T, NT = C.T, C.NT
    sb = C.sb
    st, sv, ss = C.sem("s_t"), C.sem("s_v"), C.sem("s_s")
    R = sb["rows64"]
    for tc in range(NT):
        bmean, brstd = C.ps[4 + 0], C.ps[4 + 1]
        for j, (srcoff, bank, sl) in enumerate([(0, bmean, 0), (T, brstd, 1)]):
            g = mmgrp(C, bank[:, 0:512],
                      [(ONESC(C, 1, 128, base=32), R[32:33, srcoff + tc * 512:srcoff + (tc + 1) * 512])],
                      waits=[rows_done], inc=st, bank_idx=4 + j)
            ev = op(C, "vector", lambda e, bank=bank, sl=sl: e.tensor_copy(
                sb["st32"][:, sl * 512:(sl + 1) * 512], bank[:, 0:512]),
                waits=[(st, g), (sv, sv.n)], inc=sv)
            C.psfree[4 + j] = (sv, ev)
        stats_ok = (sv, sv.n)
        for dc in range(ND):
            k = dc % 2
            sub = op(C, "vector", lambda e, dc=dc, tc=tc, k=k: e.tensor_tensor(
                sb["sc32"][:, k * 512:(k + 1) * 512], FM(C, arena, dc, tc * 512, 512),
                sb["st32"][:, 0:512], ALU.subtract), waits=[stats_ok], inc=sv)
            if gamma is None:
                op(C, "vector", lambda e, dc=dc, tc=tc, k=k: e.tensor_tensor(
                    FM(C, dst, dc, tc * 512, 512), sb["sc32"][:, k * 512:(k + 1) * 512],
                    sb["st32"][:, 512:1024], ALU.mult), waits=[(sv, sub)], inc=sv)
            else:
                m2 = op(C, "vector", lambda e, dc=dc, tc=tc, k=k: e.tensor_tensor(
                    sb["sc32"][:, k * 512:(k + 1) * 512], sb["sc32"][:, k * 512:(k + 1) * 512],
                    sb["st32"][:, 512:1024], ALU.mult), waits=[(sv, sub)], inc=sv)
                op(C, "vector", lambda e, dc=dc, tc=tc, k=k: e.tensor_scalar(
                    FM(C, dst, dc, tc * 512, 512), sb["sc32"][:, k * 512:(k + 1) * 512],
                    bias_col(C, gamma, dc), bias_col(C, beta, dc),
                    ALU.mult, ALU.add), waits=[(sv, m2)], inc=sv)


def emit_ln_primary(C, xsrc_dram):
    T = C.T
    for dc in range(ND):
        dma(C, "sync", "dx", lambda e, dc=dc: e.dma_start(
            out=FM(C, "f32a", dc, 0, T), in_=xsrc_dram[dc * 128:(dc + 1) * 128, :]))
    barrier(C)
    rows_done = emit_stats_rows(C, "f32a")
    emit_ln_apply(C, "f32a", rows_done)
    barrier(C)



def exch_flag_write(C, exch_idx, data_waits):
    win = C.dram["shflag"][0:1, 2 * exch_idx:2 * exch_idx + 2]
    dma(C, "gpsimd", "dfl", lambda e: e.dma_start(
        out=win[0:1, bass.DynSlice(C.snap_half, 1)],
        in_=C.sb["u32s"][0:1, 0:1]), waits=data_waits)


def exch_poll(C, exch_idx):
    xr = C.sem("xrdy")
    pl = C.sem(f"poll{C.poll_idx}")
    C.poll_idx += 1
    base = 0

    @C.sec("gpsimd")
    def _(e):
        with (
            e.register("cnt") as cnt,
            e.register("fl") as fl,
            e.register("nn") as nn,
            e.register("cond") as cond,
        ):
            e.reg_mov(cnt, base)
            e.reg_load(nn, C.sb["u32s"][0:1, 0:1])
            e.reg_mov(cond, 1)
            win = C.dram["shflag"][0:1, 2 * exch_idx:2 * exch_idx + 2]
            with e.While(cond):
                e.reg_alu(cnt, cnt, 16, ALU.add)
                e.dma_start(out=C.sb["u32s"][0:1, 8:9],
                            in_=win[0:1, bass.DynSlice(C.snap_ohalf, 1)]).then_inc(pl.h, 16)
                e.wait_ge(pl.h, cnt)
                e.reg_load(fl, C.sb["u32s"][0:1, 8:9])
                e.reg_alu(cond, fl, nn, ALU.not_equal)
            e.sem_inc(xr.h, 1)
        C.last["gpsimd"] = None
        C.last_update["gpsimd"] = None
    return (xr, xr.bump())



def stream_strip(C, w_ap, c0, ncols, r0, nrows):
    ring = C.ring
    out = []
    if ncols <= 128:
        for r in range(r0, r0 + nrows, 512):
            rr = min(512, r0 + nrows - r)
            ap = w_ap[r:r + rr, c0:c0 + ncols].rearrange("(a p) f -> p a f", p=128)
            slot, tgt, si = ring.load_shaped(ap, (rr // 128) * ncols, rr // 128, ncols)
            for a in range(rr // 128):
                out.append((slot[:, a * ncols:(a + 1) * ncols], tgt, si))
    else:
        assert ncols <= 512
        for r in range(r0, r0 + nrows, 128):
            ap = w_ap[r:r + 128, c0:c0 + ncols]
            slot, tgt, si = ring.load(ap, ncols)
            out.append((slot[:, 0:ncols], tgt, si))
    return out


def emit_proj_fm(C, w_ap, fc_offsets, src_arena, nk, evac, wrow0=0, banks=(0, 1),
                 src_fn=None, extra_mm_waits=()):
    st = C.sem("s_t")
    i = 0
    for fi, fc0 in enumerate(fc_offsets):
        tiles = stream_strip(C, w_ap, fc0, 128, wrow0, nk * 128)
        for tc in range(C.NT):
            b = banks[i % len(banks)]
            i += 1
            bank = C.ps[b]
            pw = []
            pairs = []
            for kc in range(nk):
                tile, tgt, _ = tiles[kc]
                w = []
                if tc == 0:
                    w.append(tgt)
                if kc == 0:
                    w.extend(extra_mm_waits)
                src = src_fn(kc, tc) if src_fn else FM(C, src_arena, kc, tc * 512, 512)
                pairs.append((tile, src))
                pw.append(w)
            g = mmgrp(C, bank[:, 0:512], pairs, per_mm_waits=pw, inc=st, bank_idx=b)
            C.psfree[b] = evac(fi, tc, bank, (st, g))
        for _, _, si in tiles:
            C.ring.consumed(si, st, st.n)


def dyncol(e, base_runtime, width):
    return bass.DynSlice(base_runtime, width)


def emit_l0_qkv(C, src="xn16"):
    T, NT = C.T, C.NT
    sb = C.sb
    st, ss, sv = C.sem("s_t"), C.sem("s_s"), C.sem("s_v")
    stage_free = {}

    def qk_evac(which):
        def evac(fi, tc, bank, mmtgt):
            sl = (fi * NT + tc) % 2
            w = [mmtgt]
            if sl in stage_free:
                w.append(stage_free[sl])
            stg = sb["qkstg"][:, sl * 512:(sl + 1) * 512]
            bcol = bias_col(C, "l0_bqkv", fi if which == "q" else 8 + fi)
            ev = op(C, "scalar", lambda e, stg=stg, bank=bank, bcol=bcol: e.activation(
                stg, bank[:, 0:512], AF.Identity, bias=bcol), waits=w, inc=ss)
            dst = C.dram["qT"] if which == "q" else C.dram["kTp"]
            stage_free[sl] = dma(C, "sync", f"dqk{sl}",
                                 lambda e, stg=stg, fi=fi, tc=tc, dst=dst: e.dma_start(
                                     out=dst[fi * 128:(fi + 1) * 128, tc * 512:(tc + 1) * 512],
                                     in_=stg), waits=[(ss, ev)])
            return (ss, ev)
        return evac

    emit_proj_fm(C, C.aps["l0_wqkv"], [i * 128 for i in range(8)], src, ND,
                 qk_evac("q"), banks=(0, 1))
    emit_proj_fm(C, C.aps["l0_wqkv"], [1024 + i * 128 for i in range(8)], src, ND,
                 qk_evac("k"), banks=(0, 1))
    vdma_prev = None
    for fh in range(2):
        tiles = stream_strip(C, C.aps["l0_wqkv"], 2048 + fh * 512, 512, 0, 1024)
        for tt in range(C.TT):
            bank = C.ps[2 + tt % 2]
            pairs, pw = [], []
            for dc in range(ND):
                tile, tgt, _ = tiles[dc]
                w = []
                if tt == 0:
                    w.append(tgt)
                pairs.append((FM(C, src, dc, tt * 128, 128), tile))
                pw.append(w)
            g = mmgrp(C, bank[:, 0:512], pairs, per_mm_waits=pw, inc=st,
                      bank_idx=2 + tt % 2)
            w = [(st, g)]
            if vdma_prev is not None:
                w.append(vdma_prev)
            ev = None
            for hh in range(8):
                h = fh * 8 + hh
                ev = op(C, "scalar", lambda e, bank=bank, h=h, hh=hh: e.activation(
                    sb["sc16"][:, h * 65:h * 65 + 64], bank[:, hh * 64:hh * 64 + 64],
                    AF.Identity), waits=(w if hh == 0 else []), inc=ss)
            C.psfree[2 + tt % 2] = (ss, ev)

            vdma_prev = dma(C, "sync", f"dv{tt % 2}",
                            lambda e, tt=tt, fh=fh: e.dma_start(
                                out=C.dram["v65p"][tt * 128:(tt + 1) * 128, fh * 520:fh * 520 + 520],
                                in_=sb["sc16"][:, fh * 520:fh * 520 + 520]),
                            waits=[(ss, ev)])
        for _, _, si in tiles:
            C.ring.consumed(si, st, st.n)
    stores_done = all_dma_waits(C)
    dma(C, "gpsimd", "dsh", lambda e: e.dma_start(
        out=C.dram["shk"][:, bass.DynSlice(C.snap_halfT, T)],
        in_=C.dram["kTp"][:, :]), waits=stores_done)
    tgt = dma(C, "gpsimd", "dsh", lambda e: e.dma_start(
        out=C.dram["shv"][bass.DynSlice(C.snap_halfT, T), :],
        in_=C.dram["v65p"][:, :]))
    return tgt


def emit_attn_core(C, exch_idx, exp_scale, exp_bias, lnbias, out_arena="b16a"):
    T, NT = C.T, C.NT
    sb = C.sb
    st, ss, sv = C.sem("s_t"), C.sem("s_s"), C.sem("s_v")
    KC = 2 * T // 128
    rdy = exch_poll(C, exch_idx)
    kvw = {}
    kcons = {}
    prev_scores = None
    for qg in range(NT):
        for hg in range(4):
            qw = [rdy] if qg == 0 and hg == 0 else [prev_scores]
            qt = dma(C, "sync", "dq", lambda e, hg=hg, qg=qg: e.dma_start(
                out=sb["qblk"][:, 0:1024].rearrange("p (a q) -> p a q", a=2, q=512),
                in_=C.dram["qT"][hg * 256:(hg + 1) * 256, qg * 512:(qg + 1) * 512]
                .rearrange("(a p) q -> p a q", p=128)), waits=qw)
            o_banks = [C.ps[i] for i in range(4)]
            for kc in range(KC):
                pb = kc % 2
                w = [rdy, qt]
                if pb in kcons:
                    w.append(kcons[pb])
                t1 = dma(C, "sync", f"dka{pb}", lambda e, hg=hg, kc=kc, pb=pb: e.dma_start(
                    out=sb["kv16"][:, pb * 256:pb * 256 + 256].rearrange(
                        "p (a q) -> p a q", a=2, q=128),
                    in_=C.dram["shk"][hg * 256:(hg + 1) * 256, kc * 128:(kc + 1) * 128]
                    .rearrange("(a p) q -> p a q", p=128)), waits=w)
                t2 = dma(C, "sync", f"dvv{pb}", lambda e, hg=hg, kc=kc, pb=pb: e.dma_start(
                    out=sb["kv16"][:, 512 + pb * 260:512 + pb * 260 + 260],
                    in_=C.dram["shv"][kc * 128:(kc + 1) * 128,
                                      hg * 260:(hg + 1) * 260]))
                kvw[pb] = [t1, t2]
                for hh in range(4):
                    sbank = C.ps[4 + hh % 2]
                    sgate = C.psfree.pop(4 + hh % 2, None)
                    kb = sb["kv16"][:, pb * 256:pb * 256 + 256]
                    lhsT = kb[(hh % 2) * 64:(hh % 2) * 64 + 64,
                              (hh // 2) * 128:(hh // 2) * 128 + 128]
                    rhs = sb["qblk"][(hh % 2) * 64:(hh % 2) * 64 + 64,
                                     (hh // 2) * 512:(hh // 2) * 512 + 512]
                    smw = list(kvw[pb])
                    if sgate is not None:
                        smw.append(sgate)
                    if (hh % 2) in C.exp_done:
                        smw.append(C.exp_done[hh % 2])
                    g = mmgrp(C, sbank[:, 0:512], [(lhsT, rhs)], waits=smw, inc=st)
                    at = sb["kv16"][:, 1032 + (hh % 2) * 512:1032 + (hh % 2) * 512 + 512]
                    ev = op(C, "scalar", lambda e, at=at, sbank=sbank: e.activation(
                        at, sbank[:, 0:512], AF.Exp, bias=exp_bias, scale=exp_scale),
                        waits=[(st, g)], inc=ss)
                    C.exp_done[hh % 2] = (ss, ev)
                    vb = sb["kv16"][:, 512 + pb * 260:512 + pb * 260 + 260]
                    avw = [(ss, ev)]
                    if kc == 0:
                        og = C.psfree.pop(hh, None)
                        if og is not None:
                            avw.append(og)
                    gg = mmgrp(C, o_banks[hh][0:65, 0:512],
                               [(vb[:, hh * 65:(hh + 1) * 65], at)],
                               waits=avw, inc=st, start_first=(kc == 0),
                               stop_last=(kc == KC - 1))
                    if hh == 3:
                        kcons[pb] = (st, gg)
                    if kc == KC - 1 and hh == 3:
                        prev_scores = (st, gg)
            for hh in range(4):
                h = hg * 4 + hh
                lnr = sb["lnrow"][0:1, (hh % 2) * 512:(hh % 2) * 512 + 512]
                lw = [(st, st.n)]
                if (hh % 2) in C.lnr_free:
                    lw.append(C.lnr_free[hh % 2])
                l1 = op(C, "scalar", lambda e, lnr=lnr, hh=hh: e.activation(
                    lnr, o_banks[hh][64:65, 0:512], AF.Ln), waits=lw, inc=ss)
                bbank = C.ps[6 + hh % 2]
                bw = [(ss, l1)]
                bg = C.psfree.pop(6 + hh % 2, None)
                if bg is not None:
                    bw.append(bg)
                gb = mmgrp(C, bbank[0:64, 0:512], [(ONESC(C, 1, 64), lnr)],
                           waits=bw, inc=st)
                C.lnr_free[hh % 2] = (st, gb)
                rb = sb["kv16"][:, 2056 + (hh % 2) * 512:2056 + (hh % 2) * 512 + 512]
                e1 = op(C, "scalar", lambda e, rb=rb, bbank=bbank: e.activation(
                    rb[0:64, 0:512], bbank[0:64, 0:512], AF.Exp, scale=-1.0,
                    bias=lnbias), waits=[(st, gb)], inc=ss)
                dst = FM(C, out_arena, h // 2, qg * 512, 512)[(h % 2) * 64:(h % 2) * 64 + 64, :]
                fv = op(C, "vector", lambda e, dst=dst, hh=hh, rb=rb: e.tensor_tensor(
                    dst, o_banks[hh][0:64, 0:512], rb[0:64, 0:512], ALU.mult),
                    waits=[(ss, e1)], inc=sv)
                C.psfree[6 + hh % 2] = (ss, e1)
                C.psfree[hh] = (sv, fv)
    barrier(C)


def emit_proj_out(C, w_ap, src_arena, evac, nk=ND, banks=(0, 1), wrow0=0):
    emit_proj_fm(C, w_ap, [i * 128 for i in range(ND)], src_arena, nk, evac,
                 wrow0=wrow0, banks=banks)

def emit_l0_sout(C):
    T, NT = C.T, C.NT
    sb = C.sb
    st, ss = C.sem("s_t"), C.sem("s_s")
    sv = C.sem("s_v")
    bvc = []
    for dc in range(ND):
        bvc.append(op(C, "vector", lambda e, dc=dc: e.tensor_copy(
            sb["bv16"][:, dc:dc + 1], bias_col(C, "l0_bqkv", 16 + dc)), inc=sv))
    stage_free = {}
    for fc in range(ND):
        tiles = stream_strip(C, C.aps["l0_wout"], fc * 128, 128, 0, 1024)
        pairs = [(tiles[dc][0], sb["bv16"][:, dc:dc + 1]) for dc in range(ND)]
        pw = [[tiles[dc][1], (sv, bvc[dc])] for dc in range(ND)]
        gb = mmgrp(C, C.ps[2][:, 0:1], pairs, per_mm_waits=pw, inc=st, bank_idx=2)
        eb = op(C, "scalar", lambda e, fc=fc: e.activation(
            sb["beff"][:, fc:fc + 1], C.ps[2][:, 0:1], AF.Identity,
            scale=1.0 / 32.0, bias=bias_col(C, "l0_bout", fc)), waits=[(st, gb)], inc=ss)
        C.psfree[2] = (ss, eb)
        for tc in range(NT):
            bank = C.ps[(fc * NT + tc) % 2]
            w0 = [(ss, eb)]
            pairs = [(tiles[dc][0], FM(C, "b16a", dc, tc * 512, 512)) for dc in range(ND)]
            g = mmgrp(C, bank[:, 0:512], pairs, waits=w0, inc=st,
                      bank_idx=(fc * NT + tc) % 2)
            sl = (fc * NT + tc) % 2
            ws = [(st, g)]
            if sl in stage_free:
                ws.append(stage_free[sl])
            ev = op(C, "scalar", lambda e, bank=bank, sl=sl, fc=fc: e.activation(
                sb["sc32"][:, sl * 512:(sl + 1) * 512], bank[:, 0:512], AF.Identity,
                bias=sb["beff"][:, fc:fc + 1]), waits=ws, inc=ss)
            C.psfree[(fc * NT + tc) % 2] = (ss, ev)
            stage_free[sl] = dma(C, "sync", f"dso{sl}",
                                 lambda e, sl=sl, fc=fc, tc=tc: e.dma_start(
                                     out=C.dram["sT"][fc * 128:(fc + 1) * 128, tc * 512:(tc + 1) * 512],
                                     in_=sb["sc32"][:, sl * 512:(sl + 1) * 512]),
                                 waits=[(ss, ev)])
        for _, _, si in tiles:
            C.ring.consumed(si, st, st.n)
    barrier(C)



LAYER_INPUTS = {
    0: ["l0_wqkv", "l0_wout", "l0_gate", "l0_wup", "l0_wdn", "l0_sup", "l0_sdn",
        "l0_ebias_rep"],
    1: ["l1_wqkv", "l1_wout", "l1_gate", "l1_ew1", "l1_ew2", "l1_ew3",
        "l1_sw1", "l1_sw2", "l1_sw3"],
    2: ["l2_wqv", "l2_wqu", "l2_wov", "l2_wou", "l2_wg", "l2_w1a", "l2_w1b", "l2_w2",
        "l2_b2m"],
    3: ["l3_wxz", "l3_wdt", "l3_wb", "l3_wc", "l3_wout", "l3_alog", "l3_aw1",
        "l3_aw2", "l3_gate", "l3_ew1", "l3_ew2", "l3_gens", "l3_gcw", "l3_eb2m",
        "oddcol"],
}
COMMON_INPUTS = ["biases", "identity", "ones_row", "onehot32", "nonce", "slot",
                 "oddmask", "oddcol", "xT"]


def declare_common_dram(C):
    C.dram["shflag"] = C.nc.dram_tensor("shflag", [1, 64], U32, kind="Internal",
                                        addr_space="Shared").ap()
    C.dram["sT"] = C.nc.dram_tensor("sT", [1024, C.T], F32, kind="Internal").ap()


def declare_layer_dram(C, L):
    nc, T = C.nc, C.T
    d = C.dram
    if L in (0, 2):
        d["qT"] = nc.dram_tensor(f"qT{L}", [1024, T], BF16, kind="Internal").ap()
        d["shk"] = nc.dram_tensor(f"shk{L}", [1024, 2 * T], BF16, kind="Internal",
                                  addr_space="Shared").ap()
        d["shv"] = nc.dram_tensor(f"shv{L}", [2 * T, 1040], BF16, kind="Internal",
                                  addr_space="Shared").ap()
        d["kTp"] = nc.dram_tensor(f"kTp{L}", [1024, T], BF16, kind="Internal").ap()
        d["v65p"] = nc.dram_tensor(f"v65p{L}", [T, 1040], BF16, kind="Internal").ap()
    if L == 1:
        d["qT"] = nc.dram_tensor("qT1", [1024, T], BF16, kind="Internal").ap()
        d["ktok"] = nc.dram_tensor("ktok1", [T, 1024], BF16, kind="Internal").ap()
        d["v65"] = nc.dram_tensor("v651", [T, 1040], BF16, kind="Internal").ap()
        d["shp"] = nc.dram_tensor("shp1", [2, 16 * 64 * 65], F32, kind="Internal",
                                  addr_space="Shared").ap()
    if L == 3:
        d["shst"] = nc.dram_tensor("shst3", [2, 16384], F32, kind="Internal",
                                   addr_space="Shared").ap()
        d["zT"] = nc.dram_tensor("zT3", [1024, T], BF16, kind="Internal").ap()
        d["xnT"] = nc.dram_tensor("xnT3", [1024, T], BF16, kind="Internal").ap()
        d["yT"] = nc.dram_tensor("yT3", [1024, T], BF16, kind="Internal").ap()
        d["abelT"] = nc.dram_tensor("abelT3", [1024, T], BF16, kind="Internal").ap()
        d["eAm"] = nc.dram_tensor("eAm3", [1024, 1024], BF16, kind="Internal").ap()
        d["eAp"] = nc.dram_tensor("eAp3", [1024, 1024], BF16, kind="Internal").ap()
        d["eH"] = nc.dram_tensor("eH3", [1024, 1024], BF16, kind="Internal").ap()
        d["eHt"] = nc.dram_tensor("eHt3", [1024, 1024], BF16, kind="Internal").ap()
        d["eG"] = nc.dram_tensor("eG3", [1024, 1024], BF16, kind="Internal").ap()
        d["eGt"] = nc.dram_tensor("eGt3", [1024, 1024], BF16, kind="Internal").ap()
        for gi in range(4):
            d[f"rot{gi}"] = nc.dram_tensor(f"rot{gi}_3", [1024, 1024], BF16,
                                           kind="Internal").ap()


def emit_layer0(C, x_in, x_out):
    emit_ln_primary(C, x_in)
    kvt = emit_l0_qkv(C)
    exch_flag_write(C, C.exch_base, [kvt])
    barrier(C)
    emit_attn_core(C, C.exch_base, exp_scale=1.0, exp_bias=-20.0,
                   lnbias=-3.4657359027997265)
    emit_l0_sout(C)
    dma(C, "sync", "deb", lambda e: e.dma_start(
        out=C.sb["ebias"][:, 0:8], in_=C.aps["l0_ebias_rep"][:, :]))
    emit_gating(C, 8, 2, C.aps["l0_gate"], "l0")
    emit_moe_l0(C)
    emit_combine(C, x_in, x_out)


def emit_gating(C, E, Ksel, gate_w_ap, mode, src="xn16", logit_bias=None,
                ebias_rep_ap=None, nk=ND, pre=False):
    T, NT, TT = C.T, C.NT, C.TT
    sb = C.sb
    st, ss, sv = C.sem("s_t"), C.sem("s_s"), C.sem("s_v")
    R = sb["rows64"]
    if pre:
        logits_done = (ss, 0)
    else:
        tiles = stream_strip(C, gate_w_ap, 0, E, 0, nk * 128)
        for tc in range(NT):
            bank = C.ps[0 if tc % 2 == 0 else 1]
            pw = [[tiles[kc][1]] if tc == 0 else [] for kc in range(nk)]
            g = mmgrp(C, bank[0:E, 0:512], [(tiles[kc][0], FM(C, src, kc, tc * 512, 512))
                                            for kc in range(nk)],
                      per_mm_waits=pw, inc=st, bank_idx=tc % 2)
            kw = {}
            if logit_bias is not None:
                kw["bias"] = bias_col(C, logit_bias)[0:E, :]
            ev = op(C, "scalar", lambda e, bank=bank, tc=tc, kw=kw: e.activation(
                R[0:E, tc * 512:(tc + 1) * 512], bank[0:E, 0:512], AF.Identity, **kw),
                waits=[(st, g)], inc=ss)
            C.psfree[tc % 2] = (ss, ev)
        for _, _, si in tiles:
            C.ring.consumed(si, st, st.n)
        logits_done = (ss, ss.n)
    for tt in range(TT):
        bank = C.ps[2 + tt % 2]
        gate = C.psfree.pop(2 + tt % 2, None)

        @C.sec("tensor")
        def _(e, tt=tt, bank=bank, gate=gate):
            e.wait_ge(logits_done[0].h, logits_done[1])
            if gate is not None:
                e.wait_ge(gate[0].h, gate[1])
            inst = e.transpose(bank[:, 0:16], R[0:16, tt * 128:(tt + 1) * 128],
                               IDN(C)[0:16, 0:16])
            inst.then_inc(st.h, 1)
            C.last["tensor"] = inst
            C.last_is_dma["tensor"] = False
            C.last_update["tensor"] = (st, st.n + 1)
        g = st.bump()
        ev = op(C, "vector", lambda e, tt=tt, bank=bank: e.tensor_copy(
            sb["tok32"][:, tt * 16:tt * 16 + 16], bank[:, 0:16]),
            waits=[(st, g)], inc=sv)
        C.psfree[2 + tt % 2] = (sv, ev)
    tm_done = (sv, sv.n)
    if mode == "l0":
        sg = op(C, "scalar", lambda e: e.activation(
            sb["tok32"][:, 256:512], sb["tok32"][:, 0:256], AF.Sigmoid),
            waits=[tm_done], inc=ss)
        scores_done = (ss, sg)
    else:
        scores_done = tm_done
    wt_done = []
    for tt in range(TT):
        Lb = sb["tok32"][:, tt * 16:tt * 16 + 16]
        Sb = sb["tok32"][:, 256 + tt * 16:256 + tt * 16 + 16]
        DUP = sb["tok32"][:, 512:512 + 2 * E]
        RK = sb["tok32"][:, 544:544 + E]
        MS = sb["tok32"][:, 560:560 + E]
        SC = sb["tok32"][:, 576:576 + E]
        w0 = [scores_done] + ([(sv, wt_done[-1])] if wt_done else [])
        if mode == "l0":
            b1 = op(C, "vector", lambda e, Sb=Sb: e.tensor_tensor(
                DUP[:, 0:E], Sb[:, 0:E], sb["ebias"][:, 0:E], ALU.add),
                waits=w0, inc=sv)
        else:
            b1 = op(C, "vector", lambda e, Lb=Lb: e.tensor_copy(
                DUP[:, 0:E], Lb[:, 0:E]), waits=w0, inc=sv)
        b2 = op(C, "vector", lambda e: e.tensor_copy(DUP[:, E:2 * E], DUP[:, 0:E]),
                waits=[(sv, b1)], inc=sv)
        z = op(C, "vector", lambda e: e.memset(RK[:, 0:E], 0.0), waits=[(sv, b2)], inc=sv)
        prev = z
        for s in range(1, E):
            nw = E - s
            c1 = op(C, "vector", lambda e, s=s, nw=nw: e.tensor_tensor(
                MS[:, 0:nw], DUP[:, s:s + nw], DUP[:, 0:nw], ALU.is_gt),
                waits=[(sv, prev)], inc=sv)
            c2 = op(C, "vector", lambda e, s=s, nw=nw: e.tensor_tensor(
                MS[:, nw:E], DUP[:, s + nw:E + s], DUP[:, nw:E], ALU.is_ge),
                waits=[(sv, c1)], inc=sv)
            prev = op(C, "vector", lambda e: e.tensor_tensor(
                RK[:, 0:E], RK[:, 0:E], MS[:, 0:E], ALU.add), waits=[(sv, c2)], inc=sv)
        mk = op(C, "vector", lambda e: e.tensor_single_scalar(
            MS[:, 0:E], RK[:, 0:E], float(Ksel), ALU.is_lt), waits=[(sv, prev)], inc=sv)
        if mode == "l0":
            wv = op(C, "vector", lambda e, Sb=Sb: e.tensor_tensor(
                SC[:, 0:E], Sb[:, 0:E], MS[:, 0:E], ALU.mult), waits=[(sv, mk)], inc=sv)
        else:
            mx = op(C, "vector", lambda e, Lb=Lb: e.tensor_reduce(
                RK[:, 0:1], Lb[:, 0:E], AX.X, ALU.max, negate=True),
                waits=[(sv, mk)], inc=sv)
            ex = op(C, "scalar", lambda e, Lb=Lb: e.activation(
                SC[:, 0:E], Lb[:, 0:E], AF.Exp, bias=RK[:, 0:1]),
                waits=[(sv, mx)], inc=ss)
            me = op(C, "vector", lambda e: e.tensor_tensor(
                SC[:, 0:E], SC[:, 0:E], MS[:, 0:E], ALU.mult), waits=[(ss, ex)], inc=sv)
            sm = op(C, "vector", lambda e: e.tensor_reduce(
                RK[:, 0:1], SC[:, 0:E], AX.X, ALU.add), waits=[(sv, me)], inc=sv)
            rc = op(C, "vector", lambda e: e.reciprocal(
                RK[:, 1:2], RK[:, 0:1]), waits=[(sv, sm)], inc=sv)
            wv = op(C, "vector", lambda e: e.tensor_scalar(
                SC[:, 0:E], SC[:, 0:E], RK[:, 1:2], 0.0, ALU.mult, ALU.add),
                waits=[(sv, rc)], inc=sv)
        bank = C.ps[2 + tt % 2]
        gate = C.psfree.pop(2 + tt % 2, None)

        @C.sec("tensor")
        def _(e, tt=tt, bank=bank, gate=gate, wv=wv):
            e.wait_ge(sv.h, wv)
            if gate is not None:
                e.wait_ge(gate[0].h, gate[1])
            inst = e.transpose(bank[0:16, 0:128], sb["tok32"][:, 576:592],
                               IDN(C)[:, 0:128])
            inst.then_inc(st.h, 1)
            C.last["tensor"] = inst
            C.last_is_dma["tensor"] = False
            C.last_update["tensor"] = (st, st.n + 1)
        g = st.bump()
        evw = op(C, "vector", lambda e, tt=tt, bank=bank: e.tensor_copy(
            R[0:16, T + tt * 128:T + (tt + 1) * 128], bank[0:16, 0:128]),
            waits=[(st, g)], inc=sv)
        C.psfree[2 + tt % 2] = (sv, evw)
        wt_done.append(evw)
    barrier(C)


def bcast_gate(C, e_idx, tc, bank_idx, dtype16=True):
    st = C.sem("s_t")
    g = mmgrp(C, C.ps[bank_idx][:, 0:512],
              [(onehot_lhsT(C, e_idx, 16),
                C.sb["rows64"][0:16, C.T + tc * 512:C.T + (tc + 1) * 512])],
              inc=st, bank_idx=bank_idx)
    return C.ps[bank_idx], (st, g)



def emit_moe_l0(C):
    T, NT = C.T, C.NT
    sb = C.sb
    st, ss, sv = C.sem("s_t"), C.sem("s_s"), C.sem("s_v")
    HC = 11
    first = [True]

    def one_expert(e_idx, wup_ap, wdn_ap, gated):
        for tc in range(NT):
            if gated:
                wrep, wg = bcast_gate(C, e_idx, tc, 6)
                cw = op(C, "scalar", lambda e, wrep=wrep: e.activation(
                    sb["gw16"][:, 0:512], wrep[:, 0:512], AF.Identity),
                    waits=[wg], inc=ss)
                C.psfree[6] = (ss, cw)
                for dc in range(ND):
                    xw = op(C, "vector", lambda e, dc=dc, tc=tc: e.tensor_tensor(
                        sb["b16a"][:, dc * 512:(dc + 1) * 512],
                        FM(C, "xn16", dc, tc * 512, 512), sb["gw16"][:, 0:512],
                        ALU.mult), waits=[(ss, cw)] if dc == 0 else [], inc=sv)
                xe_done = (sv, xw)
                src_fn = lambda kc, _tc: sb["b16a"][:, kc * 512:(kc + 1) * 512]
            else:
                xe_done = (sv, 0)
                src_fn = lambda kc, _tc: FM(C, "xn16", kc, tc * 512, 512)
            for hc in range(HC):
                tiles = C.l0_up_tiles[hc]
                bank = C.ps[hc % 2]
                pw = []
                for kc in range(ND):
                    w = [xe_done] if kc == 0 else []
                    if tc == 0:
                        w.append(tiles[kc][1])
                    pw.append(w)
                g = mmgrp(C, bank[:, 0:512],
                          [(tiles[kc][0], src_fn(kc, tc)) for kc in range(ND)],
                          per_mm_waits=pw, inc=st, bank_idx=hc % 2)
                gb = sb["moe16"][:, hc * 512:(hc + 1) * 512]
                r1 = op(C, "vector", lambda e, gb=gb, bank=bank: e.tensor_single_scalar(
                    gb, bank[:, 0:512], 0.0, ALU.max), waits=[(st, g)], inc=sv)
                C.psfree[hc % 2] = (sv, r1)
                op(C, "scalar", lambda e, gb=gb: e.activation(gb, gb, AF.Square),
                   waits=[(sv, r1)], inc=ss)
            sq_done = (ss, ss.n)
            for dc in range(ND):
                tiles = C.l0_dn_tiles[dc]
                bank = C.ps[2 + dc % 2]
                pw = []
                for hc in range(HC):
                    w = [sq_done] if hc == 0 else []
                    if tc == 0:
                        w.append(tiles[hc][1])
                    pw.append(w)
                g = mmgrp(C, bank[:, 0:512],
                          [(tiles[hc][0],
                            sb["moe16"][:, hc * 512:(hc + 1) * 512])
                           for hc in range(HC)],
                          per_mm_waits=pw, inc=st, bank_idx=2 + dc % 2)
                if first[0] and tc is not None and e_idx == 0:
                    a1 = op(C, "vector", lambda e, dc=dc, tc=tc, bank=bank: e.tensor_copy(
                        FM(C, "f32a", dc, tc * 512, 512), bank[:, 0:512]),
                        waits=[(st, g)], inc=sv)
                else:
                    a1 = op(C, "vector", lambda e, dc=dc, tc=tc, bank=bank: e.tensor_tensor(
                        FM(C, "f32a", dc, tc * 512, 512), FM(C, "f32a", dc, tc * 512, 512),
                        bank[:, 0:512], ALU.add), waits=[(st, g)], inc=sv)
                C.psfree[2 + dc % 2] = (sv, a1)

    for e_idx in range(8):
        C.l0_up_tiles = [stream_strip(C, C.aps["l0_wup"][e_idx], hc * 128, 128, 0, 1024)
                         for hc in range(HC)]
        C.l0_dn_tiles = [stream_strip(C, C.aps["l0_wdn"][e_idx], dc * 128, 128, 0, HC * 128)
                         for dc in range(ND)]
        one_expert(e_idx, None, None, True)
        first[0] = False
        for tl in C.l0_up_tiles + C.l0_dn_tiles:
            for _, _, si in tl:
                C.ring.consumed(si, st, st.n)
    C.l0_up_tiles = [stream_strip(C, C.aps["l0_sup"], hc * 128, 128, 0, 1024)
                     for hc in range(HC)]
    C.l0_dn_tiles = [stream_strip(C, C.aps["l0_sdn"], dc * 128, 128, 0, HC * 128)
                     for dc in range(ND)]
    one_expert(0, None, None, False)
    for tl in C.l0_up_tiles + C.l0_dn_tiles:
        for _, _, si in tl:
            C.ring.consumed(si, st, st.n)
    barrier(C)



def emit_combine(C, x_in, x_out):
    T, NT = C.T, C.NT
    sb = C.sb
    ss, sv = C.sem("s_s"), C.sem("s_v")
    done = None
    for dc in range(ND):
        for tc in range(NT):
            xsl = sb["sc32"][:, 0:512]
            ssl = sb["sc32"][:, 512:1024]
            w = [done] if done is not None else []
            tx = dma(C, "sync", "dcx", lambda e, dc=dc, tc=tc, xsl=xsl: e.dma_start(
                out=xsl, in_=x_in[dc * 128:(dc + 1) * 128, tc * 512:(tc + 1) * 512]),
                waits=w)
            ts_ = dma(C, "sync", "dcs", lambda e, dc=dc, tc=tc, ssl=ssl: e.dma_start(
                out=ssl, in_=C.dram["sT"][dc * 128:(dc + 1) * 128, tc * 512:(tc + 1) * 512]),
                waits=w)
            gv = op(C, "scalar", lambda e, dc=dc, tc=tc: e.activation(
                FM(C, "f32a", dc, tc * 512, 512), FM(C, "f32a", dc, tc * 512, 512),
                AF.Silu), inc=ss)
            m1 = op(C, "vector", lambda e, ssl=ssl, dc=dc, tc=tc: e.tensor_tensor(
                ssl, ssl, FM(C, "f32a", dc, tc * 512, 512), ALU.mult),
                waits=[ts_, (ss, gv)], inc=sv)
            m2 = op(C, "vector", lambda e, xsl=xsl, ssl=ssl: e.tensor_tensor(
                xsl, xsl, ssl, ALU.add), waits=[tx, (sv, m1)], inc=sv)
            done = dma(C, "sync", "dco", lambda e, dc=dc, tc=tc, xsl=xsl: e.dma_start(
                out=x_out[dc * 128:(dc + 1) * 128, tc * 512:(tc + 1) * 512], in_=xsl),
                waits=[(sv, m2)])
    barrier(C)



def emit_l2_qkv(C):
    T, NT = C.T, C.NT
    sb = C.sb
    st, ss = C.sem("s_t"), C.sem("s_s")

    def xl_fm(kc, tc):
        return sb["moe16"][:, kc * T + tc * 512:kc * T + tc * 512 + 512]

    for fi in range(2):
        tiles = stream_strip(C, C.aps["l2_wqv"], fi * 128, 128, 0, 1024)
        for tc in range(NT):
            bank = C.ps[(fi * NT + tc) % 2]
            pw = [[tiles[kc][1]] if tc == 0 else [] for kc in range(ND)]
            g = mmgrp(C, bank[:, 0:512], [(tiles[kc][0], FM(C, "xn16", kc, tc * 512, 512))
                                          for kc in range(ND)],
                      per_mm_waits=pw, inc=st, bank_idx=(fi * NT + tc) % 2)
            ev = op(C, "scalar", lambda e, bank=bank, fi=fi, tc=tc: e.activation(
                xl_fm(fi, tc), bank[:, 0:512], AF.Identity), waits=[(st, g)], inc=ss)
            C.psfree[(fi * NT + tc) % 2] = (ss, ev)
        for _, _, si in tiles:
            C.ring.consumed(si, st, st.n)
    barrier(C)
    stage_free = {}

    def qk_evac(which):
        def evac(fi, tc, bank, mmtgt):
            sl = (fi * NT + tc) % 2
            w = [mmtgt]
            if sl in stage_free:
                w.append(stage_free[sl])
            stg = sb["qkstg"][:, sl * 512:(sl + 1) * 512]
            ev = op(C, "scalar", lambda e, stg=stg, bank=bank: e.activation(
                stg, bank[:, 0:512], AF.Identity), waits=w, inc=ss)
            dst = C.dram["qT"] if which == "q" else C.dram["kTp"]
            stage_free[sl] = dma(C, "sync", f"dqk{sl}",
                                 lambda e, stg=stg, fi=fi, tc=tc, dst=dst: e.dma_start(
                                     out=dst[fi * 128:(fi + 1) * 128, tc * 512:(tc + 1) * 512],
                                     in_=stg), waits=[(ss, ev)])
            return (ss, ev)
        return evac

    emit_proj_fm(C, C.aps["l2_wqu"], [i * 128 for i in range(8)], None, 2,
                 qk_evac("q"), banks=(0, 1), src_fn=xl_fm)
    emit_proj_fm(C, C.aps["l2_wqu"], [1024 + i * 128 for i in range(8)], None, 2,
                 qk_evac("k"), banks=(0, 1), src_fn=xl_fm)
    mz = op(C, "vector", lambda e: e.memset(sb["sc16"][:, :], 1.0), inc=C.sem("s_v"))
    sv = C.sem("s_v")
    vdma_prev = None
    for fh in range(2):
        tiles = stream_strip(C, C.aps["l2_wqu"], 2048 + fh * 512, 512, 0, 256)
        for tt in range(C.TT):
            bank = C.ps[2 + tt % 2]
            pairs, pw = [], []
            for kc in range(2):
                tile, tgt, _ = tiles[kc]
                w = []
                if tt == 0:
                    w.append(tgt)
                    w.append((sv, mz))
                pairs.append((sb["moe16"][:, kc * T + tt * 128:kc * T + tt * 128 + 128], tile))
                pw.append(w)
            g = mmgrp(C, bank[:, 0:512], pairs, per_mm_waits=pw, inc=st,
                      bank_idx=2 + tt % 2)
            w = [(st, g)]
            if vdma_prev is not None:
                w.append(vdma_prev)
            ev = None
            for hh in range(8):
                h = fh * 8 + hh
                ev = op(C, "scalar", lambda e, bank=bank, h=h, hh=hh: e.activation(
                    sb["sc16"][:, h * 65:h * 65 + 64], bank[:, hh * 64:hh * 64 + 64],
                    AF.Identity), waits=(w if hh == 0 else []), inc=ss)
            C.psfree[2 + tt % 2] = (ss, ev)
            vdma_prev = dma(C, "sync", f"dv{tt % 2}",
                            lambda e, tt=tt, fh=fh: e.dma_start(
                                out=C.dram["v65p"][tt * 128:(tt + 1) * 128, fh * 520:fh * 520 + 520],
                                in_=sb["sc16"][:, fh * 520:fh * 520 + 520]),
                            waits=[(ss, ev)])
        for _, _, si in tiles:
            C.ring.consumed(si, st, st.n)
    stores_done = all_dma_waits(C)
    dma(C, "gpsimd", "dsh", lambda e: e.dma_start(
        out=C.dram["shk"][:, bass.DynSlice(C.snap_halfT, T)],
        in_=C.dram["kTp"][:, :]), waits=stores_done)
    tgt = dma(C, "gpsimd", "dsh", lambda e: e.dma_start(
        out=C.dram["shv"][bass.DynSlice(C.snap_halfT, T), :],
        in_=C.dram["v65p"][:, :]))
    return tgt


def emit_l2_sout(C):
    T, NT = C.T, C.NT
    sb = C.sb
    st, ss = C.sem("s_t"), C.sem("s_s")

    def ol_fm(kc, tc):
        return sb["moe16"][:, kc * T + tc * 512:kc * T + tc * 512 + 512]

    for fi in range(2):
        tiles = stream_strip(C, C.aps["l2_wov"], fi * 128, 128, 0, 1024)
        for tc in range(NT):
            b = 4 + (fi * NT + tc) % 2
            bank = C.ps[b]
            pw = [[tiles[kc][1]] if tc == 0 else [] for kc in range(ND)]
            g = mmgrp(C, bank[:, 0:512], [(tiles[kc][0], FM(C, "b16a", kc, tc * 512, 512))
                                          for kc in range(ND)],
                      per_mm_waits=pw, inc=st, bank_idx=b)
            ev = op(C, "scalar", lambda e, bank=bank, fi=fi, tc=tc: e.activation(
                ol_fm(fi, tc), bank[:, 0:512], AF.Identity), waits=[(st, g)], inc=ss)
            C.psfree[b] = (ss, ev)
        for _, _, si in tiles:
            C.ring.consumed(si, st, st.n)
    ol_done = (ss, ss.n)
    stage_free = {}

    def s_evac(fi, tc, bank, mmtgt):
        sl = (fi * NT + tc) % 2
        w = [mmtgt]
        if sl in stage_free:
            w.append(stage_free[sl])
        ev = op(C, "scalar", lambda e, bank=bank, sl=sl: e.activation(
            sb["sc32"][:, sl * 512:(sl + 1) * 512], bank[:, 0:512], AF.Identity),
            waits=w, inc=ss)
        stage_free[sl] = dma(C, "sync", f"dso{sl}",
                             lambda e, sl=sl, fi=fi, tc=tc: e.dma_start(
                                 out=C.dram["sT"][fi * 128:(fi + 1) * 128, tc * 512:(tc + 1) * 512],
                                 in_=sb["sc32"][:, sl * 512:(sl + 1) * 512]),
                             waits=[(ss, ev)])
        return (ss, ev)

    emit_proj_fm(C, C.aps["l2_wou"], [i * 128 for i in range(8)], None, 2,
                 s_evac, banks=(0, 1), src_fn=ol_fm, extra_mm_waits=[ol_done])
    barrier(C)


def emit_l2_rms(C):
    T, NT = C.T, C.NT
    sb = C.sb
    st, ss, sv, stq = C.sem("s_t"), C.sem("s_s"), C.sem("s_v"), C.sem("stq")
    R = sb["rows64"]
    sq_done = []
    stq0 = stq.n
    for i in range(NT * ND):
        tc, dc = divmod(i, ND)
        w = [(stq, stq0 + i - 1)] if i >= 2 else []
        sq_done.append(op(C, "scalar", lambda e, dc=dc, tc=tc, k=i % 2: e.activation(
            sb["sc32"][:, k * 512:(k + 1) * 512], FM(C, "xn16", dc, tc * 512, 512),
            AF.Square), waits=w, inc=ss))
    for tc in range(NT):
        bank2 = C.ps[2 + tc % 2]
        gate2 = C.psfree.pop(2 + tc % 2, None)

        @C.sec("tensor")
        def _(e, tc=tc, bank2=bank2, gate2=gate2):
            if gate2 is not None:
                e.wait_ge(gate2[0].h, gate2[1])
            for dc in range(ND):
                i = tc * ND + dc
                e.wait_ge(ss.h, sq_done[i])
                inst = e.matmul(bank2[0:1, 0:512], ONESC(C, 128, 1),
                                sb["sc32"][:, (i % 2) * 512:((i % 2) + 1) * 512],
                                start=(dc == 0), stop=(dc == ND - 1))
                inst.then_inc(stq.h, 1)
            C.last["tensor"] = inst
            C.last_is_dma["tensor"] = False
            C.last_update["tensor"] = (stq, stq.n + ND)
        stq.bump(ND)
        g2 = stq.n
        sqv = op(C, "vector", lambda e, tc=tc, bank2=bank2: e.tensor_single_scalar(
            R[32:33, T + tc * 512:T + (tc + 1) * 512], bank2[0:1, 0:512],
            1.0 / D, ALU.mult), waits=[(stq, g2)], inc=sv)
        C.psfree[2 + tc % 2] = (sv, sqv)
    s1 = op(C, "scalar", lambda e: e.activation(
        R[32:33, T:2 * T], R[32:33, T:2 * T], AF.Ln, bias=1e-5), waits=[(sv, sv.n)], inc=ss)
    s2 = op(C, "scalar", lambda e: e.activation(
        R[32:33, T:2 * T], R[32:33, T:2 * T], AF.Exp, scale=-0.5), waits=[(ss, s1)], inc=ss)
    rows_done = (ss, s2)
    for tc in range(NT):
        bank = C.ps[4]
        g = mmgrp(C, bank[:, 0:512],
                  [(ONESC(C, 1, 128, base=32), R[32:33, T + tc * 512:T + (tc + 1) * 512])],
                  waits=[rows_done], inc=st, bank_idx=4)
        ev = op(C, "vector", lambda e, bank=bank: e.tensor_copy(
            sb["st32"][:, 512:1024], bank[:, 0:512]), waits=[(st, g)], inc=sv)
        C.psfree[4] = (sv, ev)
        for dc in range(ND):
            op(C, "vector", lambda e, dc=dc, tc=tc: e.scalar_tensor_tensor(
                FM(C, "b16a", dc, tc * 512, 512), FM(C, "xn16", dc, tc * 512, 512),
                bias_col(C, "l2_nscale", dc), sb["st32"][:, 512:1024],
                ALU.mult, ALU.mult), waits=[(sv, ev)], inc=sv)
    barrier(C)


def emit_moe_l2(C):
    T, NT = C.T, C.NT
    sb = C.sb
    st, ss, sv = C.sem("s_t"), C.sem("s_s"), C.sem("s_v")
    C.l2silf = {}
    for dc in range(ND):
        for tc in range(NT):
            op(C, "vector", lambda e, dc=dc, tc=tc: e.tensor_copy(
                FM(C, "f32a", dc, tc * 512, 512), FM(C, "xn16", dc, tc * 512, 512)), inc=sv)
    b2t = dma(C, "sync", "db2", lambda e: e.dma_start(
        out=sb["sc32"][0:16, 0:1024], in_=C.aps["l2_b2m"][:, :]))
    cvt = op(C, "vector", lambda e: e.tensor_copy(
        sb["moe16"][0:16, 0:1024], sb["sc32"][0:16, 0:1024]), waits=[b2t], inc=sv)
    wcv = op(C, "vector", lambda e: e.tensor_copy(
        sb["moe16"][0:16, 1024:1024 + T], sb["rows64"][0:16, T:2 * T]),
        waits=[(sv, sv.n)], inc=sv)
    for dc in range(ND):
        for tc in range(NT):
            b = 4 + (dc * NT + tc) % 2
            g = mmgrp(C, C.ps[b][:, 0:512],
                      [(sb["moe16"][0:16, dc * 128:(dc + 1) * 128],
                        sb["moe16"][0:16, 1024 + tc * 512:1024 + (tc + 1) * 512])],
                      waits=[(sv, max(cvt, wcv))], inc=st, bank_idx=b)
            a1 = op(C, "vector", lambda e, dc=dc, tc=tc, b=b: e.tensor_tensor(
                FM(C, "f32a", dc, tc * 512, 512), FM(C, "f32a", dc, tc * 512, 512),
                C.ps[b][:, 0:512], ALU.add), waits=[(st, g)], inc=sv)
            C.psfree[b] = (sv, a1)
    for e_idx in range(16):
        up_a = [stream_strip(C, C.aps["l2_w1a"][e_idx], hc * 128, 128, 0, 1024)
                for hc in range(8)]
        up_b = [stream_strip(C, C.aps["l2_w1b"][e_idx], hc * 128, 128, 0, 1024)
                for hc in range(8)]
        dn = [stream_strip(C, C.aps["l2_w2"][e_idx], dc * 128, 128, 0, 1024)
              for dc in range(ND)]
        for tc in range(NT):
            wrep, wg = bcast_gate(C, e_idx, tc, 6)
            cw = op(C, "scalar", lambda e, wrep=wrep: e.activation(
                sb["gw16"][:, 0:512], wrep[:, 0:512], AF.Identity),
                waits=[wg], inc=ss)
            C.psfree[6] = (ss, cw)
            for hc in range(8):
                ba, bb = C.ps[hc % 2], C.ps[2 + hc % 2]
                pwa = [[up_a[hc][kc][1]] if tc == 0 else [] for kc in range(ND)]
                ga = mmgrp(C, ba[:, 0:512], [(up_a[hc][kc][0], FM(C, "b16a", kc, tc * 512, 512))
                                             for kc in range(ND)],
                           per_mm_waits=pwa, inc=st, bank_idx=hc % 2)
                pwb = [[up_b[hc][kc][1]] if tc == 0 else [] for kc in range(ND)]
                gb = mmgrp(C, bb[:, 0:512], [(up_b[hc][kc][0], FM(C, "b16a", kc, tc * 512, 512))
                                             for kc in range(ND)],
                           per_mm_waits=pwb, inc=st, bank_idx=2 + hc % 2)
                A32 = sb["sc32"][:, (hc % 2) * 512:(hc % 2) * 512 + 512]
                B32 = sb["st32"][:, (hc % 2) * 512:(hc % 2) * 512 + 512]
                a1 = op(C, "vector", lambda e, ba=ba, A32=A32, e_idx=e_idx, hc=hc: e.tensor_scalar(
                    A32, ba[:, 0:512], bias_col(C, "l2_b1a", e_idx * 8 + hc), 7.0,
                    ALU.add, ALU.min), waits=[(st, ga)], inc=sv)
                C.psfree[hc % 2] = (sv, a1)
                slw = [(sv, a1)]
                if (hc % 2) in C.l2silf:
                    slw.append(C.l2silf[hc % 2])
                sl1 = op(C, "scalar", lambda e, A32=A32, hc=hc: e.activation(
                    sb["moe16"][:, 4096 + (hc % 2) * 512:4096 + (hc % 2) * 512 + 512],
                    A32, AF.Silu, scale=1.702), waits=slw, inc=ss)
                b1 = op(C, "vector", lambda e, bb=bb, B32=B32, e_idx=e_idx, hc=hc: e.tensor_scalar(
                    B32, bb[:, 0:512], bias_col(C, "l2_b1b", e_idx * 8 + hc), 7.0,
                    ALU.add, ALU.min), waits=[(st, gb)], inc=sv)
                C.psfree[2 + hc % 2] = (sv, b1)
                b2 = op(C, "vector", lambda e, B32=B32: e.tensor_scalar(
                    B32, B32, -7.0, 1.0, ALU.max, ALU.add), waits=[(sv, b1)], inc=sv)
                s5 = op(C, "vector", lambda e, B32=B32, hc=hc: e.scalar_tensor_tensor(
                    sb["moe16"][:, hc * 512:(hc + 1) * 512],
                    B32, 1.0 / 1.702,
                    sb["moe16"][:, 4096 + (hc % 2) * 512:4096 + (hc % 2) * 512 + 512],
                    ALU.mult, ALU.mult), waits=[(sv, b2), (ss, sl1)], inc=sv)
                C.l2silf[hc % 2] = (sv, s5)
                op(C, "vector", lambda e, hc=hc: e.tensor_tensor(
                    sb["moe16"][:, hc * 512:(hc + 1) * 512],
                    sb["moe16"][:, hc * 512:(hc + 1) * 512],
                    sb["gw16"][:, 0:512], ALU.mult), waits=[(sv, s5), (ss, cw)], inc=sv)
            glu_done = (sv, sv.n)
            for dc in range(ND):
                b = 4 + dc % 2
                pw = []
                for hc in range(8):
                    w = [glu_done] if hc == 0 else []
                    if tc == 0:
                        w.append(dn[dc][hc][1])
                    pw.append(w)
                g = mmgrp(C, C.ps[b][:, 0:512],
                          [(dn[dc][hc][0], sb["moe16"][:, hc * 512:(hc + 1) * 512])
                           for hc in range(8)],
                          per_mm_waits=pw, inc=st, bank_idx=b)
                a1 = op(C, "vector", lambda e, dc=dc, tc=tc, b=b: e.tensor_tensor(
                    FM(C, "f32a", dc, tc * 512, 512), FM(C, "f32a", dc, tc * 512, 512),
                    C.ps[b][:, 0:512], ALU.add), waits=[(st, g)], inc=sv)
                C.psfree[b] = (sv, a1)
        for tl in up_a + up_b + dn:
            for _, _, si in tl:
                C.ring.consumed(si, st, st.n)
    barrier(C)


def emit_layer2(C, x_in, x_out):
    emit_ln_primary(C, x_in)
    kvt = emit_l2_qkv(C)
    exch_flag_write(C, C.exch_base, [kvt])
    barrier(C)
    emit_attn_core(C, C.exch_base, exp_scale=0.125, exp_bias=0.0, lnbias=0.0)
    emit_l2_sout(C)
    emit_l2_rms(C)
    emit_gating(C, 16, 4, C.aps["l2_wg"], "softmax", src="b16a", logit_bias="l2_bg")
    emit_moe_l2(C)
    emit_combine(C, x_in, x_out)


def elu_evac(C, bank, dst16, sl):
    sb = C.sb
    ss, sv = C.sem("s_s"), C.sem("s_v")
    a = sb["sc32"][:, sl * 512:(sl + 1) * 512]
    v1 = op(C, "vector", lambda e: e.tensor_single_scalar(
        a, bank[:, 0:512], 0.0, ALU.min), inc=sv)
    s1 = op(C, "scalar", lambda e: e.activation(a, a, AF.Exp), waits=[(sv, v1)], inc=ss)
    v2 = op(C, "vector", lambda e: e.scalar_tensor_tensor(
        dst16, bank[:, 0:512], 0.0, a, ALU.max, ALU.add), waits=[(ss, s1)], inc=sv)
    return (sv, v2)


def emit_l1_qkv(C, src="b16a"):
    T, NT = C.T, C.NT
    sb = C.sb
    st, ss, sv = C.sem("s_t"), C.sem("s_s"), C.sem("s_v")
    stage_free = {}

    def q_evac(fi, tc, bank, mmtgt):
        sl = (fi * NT + tc) % 2
        w = [mmtgt]
        if sl in stage_free:
            w.append(stage_free[sl])
        stg = sb["qkstg"][:, sl * 512:(sl + 1) * 512]
        ev = elu_evac(C, bank, stg, sl)
        stage_free[sl] = dma(C, "sync", f"dqk{sl}",
                             lambda e, stg=stg, fi=fi, tc=tc: e.dma_start(
                                 out=C.dram["qT"][fi * 128:(fi + 1) * 128, tc * 512:(tc + 1) * 512],
                                 in_=stg), waits=[ev])
        return ev

    def q_evac2(fi, tc, bank, mmtgt):
        sl = (fi * NT + tc) % 2
        sb_ = C.sb
        ss, sv = C.sem("s_s"), C.sem("s_v")
        a = sb_["sc32"][:, sl * 512:(sl + 1) * 512]
        stg = sb_["qkstg"][:, sl * 512:(sl + 1) * 512]
        w = [mmtgt]
        if sl in stage_free:
            w.append(stage_free[sl])
        v1 = op(C, "vector", lambda e: e.tensor_single_scalar(
            a, bank[:, 0:512], 0.0, ALU.min), waits=w, inc=sv)
        s1 = op(C, "scalar", lambda e: e.activation(a, a, AF.Exp), waits=[(sv, v1)], inc=ss)
        v2 = op(C, "vector", lambda e: e.scalar_tensor_tensor(
            stg, bank[:, 0:512], 0.0, a, ALU.max, ALU.add), waits=[(ss, s1)], inc=sv)
        stage_free[sl] = dma(C, "sync", f"dqk{sl}",
                             lambda e: e.dma_start(
                                 out=C.dram["qT"][fi * 128:(fi + 1) * 128, tc * 512:(tc + 1) * 512],
                                 in_=stg), waits=[(sv, v2)])
        return (sv, v2)

    emit_proj_fm(C, C.aps["l1_wqkv"], [i * 128 for i in range(8)], src, ND,
                 q_evac2, banks=(0, 1))
    for which, c0, dst in [("k", 1024, "ktok"), ("v", 2048, "v65")]:
        vdma_prev = None
        for fh in range(2):
            tiles = stream_strip(C, C.aps["l1_wqkv"], c0 + fh * 512, 512, 0, 1024)
            for tt in range(C.TT):
                b = 2 + tt % 2
                bank = C.ps[b]
                pairs, pw = [], []
                for dc in range(ND):
                    tile, tgt, _ = tiles[dc]
                    w = []
                    if tt == 0:
                        w.append(tgt)
                    pairs.append((FM(C, src, dc, tt * 128, 128), tile))
                    pw.append(w)
                g = mmgrp(C, bank[:, 0:512], pairs, per_mm_waits=pw, inc=st, bank_idx=b)
                w = [(st, g)]
                if vdma_prev is not None:
                    w.append(vdma_prev)
                if which == "k":
                    sl = tt % 2
                    a = sb["sc32"][:, sl * 512:(sl + 1) * 512]
                    stg = sb["qkstg"][:, sl * 512:(sl + 1) * 512]
                    v1 = op(C, "vector", lambda e, a=a, bank=bank: e.tensor_single_scalar(
                        a, bank[:, 0:512], 0.0, ALU.min), waits=w, inc=sv)
                    s1 = op(C, "scalar", lambda e, a=a: e.activation(a, a, AF.Exp),
                            waits=[(sv, v1)], inc=ss)
                    ev = op(C, "vector", lambda e, stg=stg, a=a, bank=bank: e.scalar_tensor_tensor(
                        stg, bank[:, 0:512], 0.0, a, ALU.max, ALU.add),
                        waits=[(ss, s1)], inc=sv)
                    C.psfree[b] = (sv, ev)
                    vdma_prev = dma(C, "sync", f"dv{tt % 2}",
                                    lambda e, tt=tt, fh=fh, stg=stg: e.dma_start(
                                        out=C.dram["ktok"][tt * 128:(tt + 1) * 128,
                                                           fh * 512:(fh + 1) * 512],
                                        in_=stg), waits=[(sv, ev)])
                else:
                    ev = None
                    for hh in range(8):
                        h = fh * 8 + hh
                        ev = op(C, "scalar", lambda e, bank=bank, h=h, hh=hh: e.activation(
                            sb["sc16"][:, h * 65:h * 65 + 64], bank[:, hh * 64:hh * 64 + 64],
                            AF.Identity), waits=(w if hh == 0 else []), inc=ss)
                    C.psfree[b] = (ss, ev)
                    vdma_prev = dma(C, "sync", f"dv{tt % 2}",
                                    lambda e, tt=tt, fh=fh: e.dma_start(
                                        out=C.dram["v65"][tt * 128:(tt + 1) * 128,
                                                          fh * 520:fh * 520 + 520],
                                        in_=sb["sc16"][:, fh * 520:fh * 520 + 520]),
                                    waits=[(ss, ev)])
            for _, _, si in tiles:
                C.ring.consumed(si, st, st.n)
    barrier(C)


def emit_l1_attn(C):
    T, NT, TT = C.T, C.NT, C.TT
    sb = C.sb
    st, ss, sv = C.sem("s_t"), C.sem("s_s"), C.sem("s_v")
    for hg in range(2):
        for tt in range(TT):
            pb = tt % 2
            w = []
            if tt >= 2:
                w.append(C.l1kv_free.get(pb, (st, 0)))
            tk = dma(C, "sync", f"dka{pb}", lambda e, tt=tt, pb=pb: e.dma_start(
                out=sb["kv16"][:, pb * 1024:pb * 1024 + 1024],
                in_=C.dram["ktok"][tt * 128:(tt + 1) * 128, :]), waits=w)
            tv = dma(C, "sync", f"dvv{pb}", lambda e, tt=tt, pb=pb, hg=hg: e.dma_start(
                out=sb["kv16"][:, 2048 + pb * 520:2048 + pb * 520 + 520],
                in_=C.dram["v65"][tt * 128:(tt + 1) * 128, hg * 520:(hg + 1) * 520]),
                waits=w)
            last = None
            for hh in range(8):
                h = hg * 8 + hh
                g = mmgrp(C, C.ps[hh][0:64, 0:65],
                          [(sb["kv16"][:, pb * 1024 + h * 64:pb * 1024 + h * 64 + 64],
                            sb["kv16"][:, 2048 + pb * 520 + hh * 65:2048 + pb * 520 + hh * 65 + 65])],
                          waits=[tk, tv] + ([C.l1psg[hh]] if tt == 0 and hh in C.l1psg else []),
                          inc=st, start_first=(tt == 0), stop_last=(tt == TT - 1))
                last = g
            C.l1kv_free[pb] = (st, last)
        for hh in range(8):
            h = hg * 8 + hh
            ev = op(C, "vector", lambda e, hh=hh, h=h: e.tensor_copy(
                sb["kvacc"][0:64, h * 65:h * 65 + 65], C.ps[hh][0:64, 0:65]),
                waits=[(st, st.n)], inc=sv)
            C.l1psg[hh] = (sv, ev)
    myp = dma(C, "gpsimd", "dsh", lambda e: e.dma_start(
        out=C.dram["shp"][bass.DynSlice(C.snap_half, 1), :].rearrange(
            "1 (p f) -> p f", p=64),
        in_=sb["kvacc"][0:64, 0:1040]), waits=[(sv, sv.n)])
    exch_flag_write(C, C.exch_base, [myp])
    rdy = exch_poll(C, C.exch_base)
    tpp = dma(C, "gpsimd", "dpp", lambda e: e.dma_start(
        out=sb["f32a"][0:64, 0:1040],
        in_=C.dram["shp"][bass.DynSlice(C.snap_ohalf, 1), :].rearrange(
            "1 (p f) -> p f", p=64)), waits=[rdy])
    ad = op(C, "vector", lambda e: e.tensor_tensor(
        sb["kvacc"][0:64, 0:1040], sb["kvacc"][0:64, 0:1040],
        sb["f32a"][0:64, 0:1040], ALU.add), waits=[tpp], inc=sv)
    cv = op(C, "vector", lambda e: e.tensor_copy(
        sb["sc16"][0:64, 0:1040], sb["kvacc"][0:64, 0:1040]), waits=[(sv, ad)], inc=sv)
    kv_ready = (sv, cv)
    for h in range(16):
        for tc in range(NT):
            i = (h * NT + tc) % 2
            qt = dma(C, "sync", f"dq{i}", lambda e, h=h, tc=tc, i=i: e.dma_start(
                out=sb["qblk"][0:64, i * 512:(i + 1) * 512],
                in_=C.dram["qT"][(h % 16) * 64:(h % 16) * 64 + 64, tc * 512:(tc + 1) * 512]
                if False else
                C.dram["qT"][h * 64:h * 64 + 64, tc * 512:(tc + 1) * 512]),
                waits=[C.l1qg.get(i, kv_ready)])
            b = i
            g = mmgrp(C, C.ps[b][0:65, 0:512],
                      [(sb["sc16"][0:64, h * 65:h * 65 + 65],
                        sb["qblk"][0:64, i * 512:(i + 1) * 512])],
                      waits=[qt, kv_ready] + ([C.psfree.pop(b)] if b in C.psfree else []),
                      inc=st, bank_idx=None)
            lnr = sb["lnrow"][0:1, i * 512:(i + 1) * 512]
            l1 = op(C, "scalar", lambda e, b=b, lnr=lnr: e.activation(
                lnr, C.ps[b][64:65, 0:512], AF.Ln, bias=1e-6), waits=[(st, g)], inc=ss)
            bb = 2 + i
            gb = mmgrp(C, C.ps[bb][0:64, 0:512], [(ONESC(C, 1, 64), lnr)],
                       waits=[(ss, l1)] + ([C.psfree.pop(bb)] if bb in C.psfree else []),
                       inc=st)
            rb = sb["kv16"][:, 2056 + i * 512:2056 + i * 512 + 512]
            e1 = op(C, "scalar", lambda e, rb=rb, bb=bb: e.activation(
                rb[0:64, 0:512], C.ps[bb][0:64, 0:512], AF.Exp, scale=-1.0, bias=0.0),
                waits=[(st, gb)], inc=ss)
            C.psfree[bb] = (ss, e1)
            dst = FM(C, "b16a", h // 2, tc * 512, 512)[(h % 2) * 64:(h % 2) * 64 + 64, :]
            fv = op(C, "vector", lambda e, dst=dst, b=b, rb=rb: e.tensor_tensor(
                dst, C.ps[b][0:64, 0:512], rb[0:64, 0:512], ALU.mult),
                waits=[(ss, e1)], inc=sv)
            C.psfree[b] = (sv, fv)
            C.l1qg[i] = (sv, fv)
    barrier(C)
    stage_free = {}

    def s_evac(fi, tc, bank, mmtgt):
        sl = (fi * NT + tc) % 2
        w = [mmtgt]
        if sl in stage_free:
            w.append(stage_free[sl])
        ev = op(C, "vector", lambda e, bank=bank, sl=sl, fi=fi, tc=tc: e.tensor_tensor(
            sb["sc32"][:, sl * 512:(sl + 1) * 512], bank[:, 0:512],
            FM(C, "xn16", fi, tc * 512, 512), ALU.add), waits=w, inc=sv)
        stage_free[sl] = dma(C, "sync", f"dso{sl}",
                             lambda e, sl=sl, fi=fi, tc=tc: e.dma_start(
                                 out=C.dram["sT"][fi * 128:(fi + 1) * 128, tc * 512:(tc + 1) * 512],
                                 in_=sb["sc32"][:, sl * 512:(sl + 1) * 512]),
                             waits=[(sv, ev)])
        return (sv, ev)

    emit_proj_fm(C, C.aps["l1_wout"], [i * 128 for i in range(8)], "b16a", ND,
                 s_evac, banks=(0, 1))
    barrier(C)


def emit_moe_l1(C):
    T, NT = C.T, C.NT
    sb = C.sb
    st, ss, sv = C.sem("s_t"), C.sem("s_s"), C.sem("s_v")
    HC = 16
    first = [True]

    def one_expert(e_idx, w1ap, w2ap, w3ap, gated):
        for tc in range(NT):
            if gated:
                wrep, wg = bcast_gate(C, e_idx, tc, 6)
                cw = op(C, "scalar", lambda e, wrep=wrep: e.activation(
                    sb["gw16"][:, 0:512], wrep[:, 0:512], AF.Identity),
                    waits=[wg], inc=ss)
                C.psfree[6] = (ss, cw)
            for hc in range(HC):
                t1 = stream_strip(C, w1ap, hc * 128, 128, 0, 1024)
                t2 = stream_strip(C, w2ap, hc * 128, 128, 0, 1024)
                ba, bb = C.ps[hc % 2], C.ps[2 + hc % 2]
                pw1 = [[t1[kc][1]] for kc in range(ND)]
                g1 = mmgrp(C, ba[:, 0:512], [(t1[kc][0], FM(C, "xn16", kc, tc * 512, 512))
                                             for kc in range(ND)],
                           per_mm_waits=pw1, inc=st, bank_idx=hc % 2)
                pw2 = [[t2[kc][1]] for kc in range(ND)]
                g2 = mmgrp(C, bb[:, 0:512], [(t2[kc][0], FM(C, "xn16", kc, tc * 512, 512))
                                             for kc in range(ND)],
                           per_mm_waits=pw2, inc=st, bank_idx=2 + hc % 2)
                S16 = sb["b16a"][:, 6656 + (hc % 2) * 512:6656 + (hc % 2) * 512 + 512]
                s16w = [(st, g1)]
                if (hc % 2) in C.l1s16f:
                    s16w.append(C.l1s16f[hc % 2])
                sl1 = op(C, "scalar", lambda e, ba=ba, S16=S16: e.activation(
                    S16, ba[:, 0:512], AF.Silu), waits=s16w, inc=ss)
                C.psfree[hc % 2] = (ss, sl1)
                H2 = sb["moe16"][:, 9216 + (hc % 2) * 256:9216 + (hc % 2) * 256 + 256] \
                    if False else sb["qkstg"][:, (hc % 2) * 512:(hc % 2) * 512 + 512]
                if gated:
                    h2 = op(C, "vector", lambda e, bb=bb, H2=H2: e.tensor_tensor(
                        H2, bb[:, 0:512], sb["gw16"][:, 0:512], ALU.mult),
                        waits=[(st, g2), (ss, cw)], inc=sv)
                else:
                    h2 = op(C, "vector", lambda e, bb=bb, H2=H2: e.tensor_copy(
                        H2, bb[:, 0:512]), waits=[(st, g2)], inc=sv)
                C.psfree[2 + hc % 2] = (sv, h2)
                gdst = (sb["moe16"][:, hc * 512:(hc + 1) * 512] if hc < 11 else
                        sb["b16a"][:, 4096 + (hc - 11) * 512:4096 + (hc - 10) * 512])
                gm = op(C, "vector", lambda e, S16=S16, H2=H2, gdst=gdst: e.tensor_tensor(
                    gdst, S16, H2, ALU.mult), waits=[(sv, h2), (ss, sl1)], inc=sv)
                C.l1s16f[hc % 2] = (sv, gm)
                for tl in (t1, t2):
                    for _, _, si in tl:
                        C.ring.consumed(si, st, st.n)
            g_done = (sv, sv.n)
            for dc in range(ND):
                t3 = stream_strip(C, w3ap, dc * 128, 128, 0, HC * 128)
                b = 4 + dc % 2
                pw = []
                for hc in range(HC):
                    w = [g_done] if hc == 0 else []
                    w.append(t3[hc][1])
                    pw.append(w)
                g = mmgrp(C, C.ps[b][:, 0:512],
                          [(t3[hc][0],
                            sb["moe16"][:, hc * 512:(hc + 1) * 512] if hc < 11 else
                            sb["b16a"][:, 4096 + (hc - 11) * 512:4096 + (hc - 10) * 512])
                           for hc in range(HC)],
                          per_mm_waits=pw, inc=st, bank_idx=b)
                if first[0]:
                    a1 = op(C, "vector", lambda e, dc=dc, tc=tc, b=b: e.tensor_copy(
                        FM(C, "f32a", dc, tc * 512, 512), C.ps[b][:, 0:512]),
                        waits=[(st, g)], inc=sv)
                else:
                    a1 = op(C, "vector", lambda e, dc=dc, tc=tc, b=b: e.tensor_tensor(
                        FM(C, "f32a", dc, tc * 512, 512), FM(C, "f32a", dc, tc * 512, 512),
                        C.ps[b][:, 0:512], ALU.add), waits=[(st, g)], inc=sv)
                C.psfree[b] = (sv, a1)
                for _, _, si in t3:
                    C.ring.consumed(si, st, st.n)
        first[0] = False

    for e_idx in range(8):
        one_expert(e_idx, C.aps["l1_ew1"][e_idx], C.aps["l1_ew2"][e_idx],
                   C.aps["l1_ew3"][e_idx], True)
    one_expert(0, C.aps["l1_sw1"], C.aps["l1_sw2"], C.aps["l1_sw3"], False)
    barrier(C)


def emit_layer1(C, x_in, x_out):
    emit_ln_primary(C, x_in)
    rows_done = emit_stats_rows(C, "xn16", is16=True)
    emit_ln_apply(C, "xn16", rows_done, dst="b16a", gamma="l1_gamma", beta="l1_beta")
    barrier(C)
    C.l1kv_free, C.l1psg, C.l1qg, C.l1s16f = {}, {}, {}, {}
    emit_l1_qkv(C, src="b16a")
    emit_l1_attn(C)
    emit_gating(C, 8, 2, C.aps["l1_gate"], "softmax", src="xn16")
    emit_moe_l1(C)
    emit_combine(C, x_in, x_out)


EXPM_C = [1.0 / 720, 1.0 / 120, 1.0 / 24, 1.0 / 6, 0.5, 1.0, 1.0]


def emit_l3_gate_logits(C):
    T, NT = C.T, C.NT
    sb = C.sb
    st, ss, sv = C.sem("s_t"), C.sem("s_s"), C.sem("s_v")
    R = sb["rows64"]
    E = 6
    W = sb["tok32"]
    wld = []
    for dc in range(ND):
        wld.append(dma(C, "sync", "dgw", lambda e, dc=dc: e.dma_start(
            out=W[:, 592 + dc * E:592 + (dc + 1) * E],
            in_=C.aps["l3_gate"][dc * 128:(dc + 1) * 128, :])))
    gbW = mmgrp(C, C.ps[6][0:E, 0:1],
                [(W[:, 592 + dc * E:592 + (dc + 1) * E], bias_col(C, "l3_lnb", dc))
                 for dc in range(ND)],
                per_mm_waits=[[wld[dc]] for dc in range(ND)], inc=st, bank_idx=6)
    ebW = op(C, "vector", lambda e: e.tensor_copy(
        sb["aux2"][0:E, 1:2], C.ps[6][0:E, 0:1]), waits=[(st, gbW)], inc=sv)
    C.psfree[6] = (sv, ebW)
    ebg = op(C, "vector", lambda e: e.tensor_tensor(
        sb["aux2"][0:E, 2:3], sb["aux2"][0:E, 1:2],
        sb["bias32"][0:E, C.bias_reg["l3_gateb"][0]:C.bias_reg["l3_gateb"][0] + 1],
        ALU.add), waits=[(sv, ebW)], inc=sv)
    wsc = []
    for dc in range(ND):
        wsc.append(op(C, "vector", lambda e, dc=dc: e.tensor_scalar(
            W[:, 592 + dc * E:592 + (dc + 1) * E], W[:, 592 + dc * E:592 + (dc + 1) * E],
            bias_col(C, "l3_lng", dc), None, ALU.mult), waits=[(st, gbW)], inc=sv))
    ggs = mmgrp(C, C.ps[7][0:E, 0:1],
                [(W[:, 592 + dc * E:592 + (dc + 1) * E], ONESC(C, 128, 1))
                 for dc in range(ND)],
                per_mm_waits=[[(sv, wsc[dc])] for dc in range(ND)], inc=st, bank_idx=7)
    egs = op(C, "vector", lambda e: e.tensor_copy(
        sb["aux2"][0:E, 0:1], C.ps[7][0:E, 0:1]), waits=[(st, ggs)], inc=sv)
    C.psfree[7] = (sv, egs)
    a1 = op(C, "vector", lambda e: e.tensor_tensor(
        sb["aux2"][32:33, 0:T], R[32:33, T:2 * T], R[32:33, T:2 * T], ALU.mult),
        inc=sv)
    a2 = op(C, "vector", lambda e: e.tensor_scalar(
        sb["aux2"][32:33, 0:T], sb["aux2"][32:33, 0:T], -1e-5, 1.0 + 1e-5,
        ALU.mult, ALU.add), waits=[(sv, a1)], inc=sv)
    a3 = op(C, "scalar", lambda e: e.activation(
        sb["aux2"][32:33, 0:T], sb["aux2"][32:33, 0:T], AF.Ln), waits=[(sv, a2)], inc=ss)
    a4 = op(C, "scalar", lambda e: e.activation(
        sb["aux2"][32:33, 0:T], sb["aux2"][32:33, 0:T], AF.Exp, scale=-0.5),
        waits=[(ss, a3)], inc=ss)
    a5 = op(C, "vector", lambda e: e.tensor_tensor(
        sb["aux2"][32:33, 0:T], sb["aux2"][32:33, 0:T], R[32:33, T:2 * T], ALU.mult),
        waits=[(ss, a4)], inc=sv)
    for tc in range(NT):
        b = tc % 2
        gx = mmgrp(C, C.ps[b][0:E, 0:512],
                   [(W[:, 592 + kc * E:592 + (kc + 1) * E],
                     FM(C, "f32a", kc, tc * 512, 512)) for kc in range(ND)],
                   per_mm_waits=[[(sv, wsc[kc])] for kc in range(ND)],
                   inc=st, bank_idx=b)
        gm1 = mmgrp(C, C.ps[4][0:E, 0:512],
                    [(ONESC(C, 1, E, base=32),
                      R[32:33, tc * 512:(tc + 1) * 512])], inc=st, bank_idx=4)
        gr12 = mmgrp(C, C.ps[5][0:E, 0:512],
                     [(ONESC(C, 1, E, base=32),
                       sb["aux2"][32:33, tc * 512:(tc + 1) * 512])],
                     waits=[(sv, a5)], inc=st, bank_idx=5)
        v1 = op(C, "vector", lambda e: e.tensor_scalar(
            sb["st32"][0:E, 0:512], C.ps[4][0:E, 0:512], sb["aux2"][0:E, 0:1],
            None, ALU.mult), waits=[(st, gm1), (sv, egs)], inc=sv)
        C.psfree[4] = (sv, v1)
        v2 = op(C, "vector", lambda e, b=b: e.tensor_tensor(
            sb["st32"][0:E, 512:1024], C.ps[b][0:E, 0:512], sb["st32"][0:E, 0:512],
            ALU.subtract), waits=[(st, gx), (sv, v1)], inc=sv)
        C.psfree[b] = (sv, v2)
        v3 = op(C, "vector", lambda e: e.tensor_tensor(
            sb["st32"][0:E, 512:1024], sb["st32"][0:E, 512:1024], C.ps[5][0:E, 0:512],
            ALU.mult), waits=[(sv, v2), (st, gr12)], inc=sv)
        C.psfree[5] = (sv, v3)
        op(C, "vector", lambda e, tc=tc: e.tensor_scalar(
            R[0:E, tc * 512:(tc + 1) * 512], sb["st32"][0:E, 512:1024],
            sb["aux2"][0:E, 2:3], None, ALU.add), waits=[(sv, v3), (sv, ebg)], inc=sv)
    barrier(C)


def emit_l3_expm(C):
    sb = C.sb
    st, ss, sv = C.sem("s_t"), C.sem("s_s"), C.sem("s_v")
    c6, c5 = EXPM_C[0], EXPM_C[1]
    for gi in range(4):
        gap = C.aps["l3_gens"][gi]
        pdone = []
        for rc in range(8):
            tg = dma(C, "sync", "dxg", lambda e, rc=rc, gap=gap: e.dma_start(
                out=sb["sc32"][:, 0:1024], in_=gap[rc * 128:(rc + 1) * 128, :]),
                waits=pdone)
            pdone = []
            for name, scale in [("eAm", -0.25), ("eAp", 0.25),
                                ("eH", c6 / 4.0), ("eHt", -c6 / 4.0)]:
                v1 = op(C, "vector", lambda e, scale=scale: e.tensor_single_scalar(
                    sb["qkstg"][:, 0:1024], sb["sc32"][:, 0:1024], scale, ALU.mult),
                    waits=[tg] + pdone, inc=sv)
                if name in ("eH", "eHt"):
                    v1 = op(C, "vector", lambda e, rc=rc: e.scalar_tensor_tensor(
                        sb["qkstg"][:, rc * 128:rc * 128 + 128], IDN(C), c5,
                        sb["qkstg"][:, rc * 128:rc * 128 + 128], ALU.mult, ALU.add),
                        waits=[(sv, v1)], inc=sv)
                td = dma(C, "sync", "dyg", lambda e, name=name, rc=rc: e.dma_start(
                    out=C.dram[name][rc * 128:(rc + 1) * 128, :],
                    in_=sb["qkstg"][:, 0:1024]), waits=[(sv, v1)])
                pdone = [td]
        barrier(C)

        def matpass(lhs_name, rhs_name, dst_name, ck=None, negate=False):
            prev_dma = {}
            for jc in range(2):
                rw = [prev_dma[s] for s in prev_dma] if jc else []
                rt = dma(C, "gpsimd", "dmrhs", lambda e, jc=jc: e.dma_start(
                    out=sb["moe16"][:, 0:4096].rearrange("p (a f) -> p a f", a=8, f=512),
                    in_=C.dram[rhs_name][:, jc * 512:(jc + 1) * 512]
                    .rearrange("(a p) f -> p a f", p=128)), waits=rw)
                for ic in range(8):
                    lt = stream_strip(C, C.dram[lhs_name], ic * 128, 128, 0, 1024)
                    b = ic % 2
                    pw = []
                    for kc in range(8):
                        w = [rt] if kc == 0 else []
                        if kc == 0:
                            w.append(lt[0][1])
                        if kc == 4:
                            w.append(lt[4][1])
                        pw.append(w)
                    g = mmgrp(C, C.ps[b][:, 0:512],
                              [(lt[kc][0], sb["moe16"][:, kc * 512:(kc + 1) * 512])
                               for kc in range(8)],
                              per_mm_waits=pw, inc=st, bank_idx=b)
                    sl = b
                    w2 = [(st, g)] + ([prev_dma[sl]] if sl in prev_dma else [])
                    ev = op(C, "scalar", lambda e, b=b, sl=sl: e.activation(
                        sb["qkstg"][:, sl * 512:(sl + 1) * 512], C.ps[b][:, 0:512],
                        AF.Identity, scale=(-1.0 if negate else 1.0)),
                        waits=w2, inc=ss)
                    C.psfree[b] = (ss, ev)
                    sem, evt = ss, ev
                    if ck is not None and ic * 128 // 512 == jc:
                        sub = ic * 128 - jc * 512
                        evt = op(C, "vector", lambda e, sl=sl, sub=sub, ck=ck: e.scalar_tensor_tensor(
                            sb["qkstg"][:, sl * 512 + sub:sl * 512 + sub + 128],
                            IDN(C), ck,
                            sb["qkstg"][:, sl * 512 + sub:sl * 512 + sub + 128],
                            ALU.mult, ALU.add), waits=[(ss, ev)], inc=sv)
                        sem = sv
                    prev_dma[sl] = dma(C, "sync", f"dzg{sl}",
                                       lambda e, ic=ic, jc=jc, sl=sl, dst_name=dst_name: e.dma_start(
                                           out=C.dram[dst_name][ic * 128:(ic + 1) * 128,
                                                                jc * 512:(jc + 1) * 512],
                                           in_=sb["qkstg"][:, sl * 512:(sl + 1) * 512]),
                                       waits=[(sem, evt)])
                    for _, _, si in lt:
                        C.ring.consumed(si, st, st.n)
            barrier(C)

        cur, nxt = "eH", "eG"
        for k in range(2, 7):
            matpass("eAm", cur, nxt, ck=EXPM_C[k])
            cur, nxt = nxt, cur
        curt, nxtt = "eHt", "eGt"
        for k in range(2, 7):
            matpass("eAp", curt, nxtt, ck=EXPM_C[k])
            curt, nxtt = nxtt, curt
        matpass(curt, cur, nxt)
        matpass(cur, curt, nxtt)
        matpass(nxtt, nxt, f"rot{gi}")
    barrier(C)


def emit_l3_ssm(C):
    T, NT = C.T, C.NT
    sb = C.sb
    st, ss, sv = C.sem("s_t"), C.sem("s_s"), C.sem("s_v")
    CH = 256
    NCH = T // CH

    stage_free = {}

    def xz_evac(which):
        def evac(fi, tc, bank, mmtgt):
            if which == "x":
                ev = op(C, "scalar", lambda e, bank=bank, fi=fi, tc=tc: e.activation(
                    FM(C, "b16a", fi, tc * 512, 512), bank[:, 0:512], AF.Identity),
                    waits=[mmtgt], inc=ss)
                return (ss, ev)
            sl = (fi * NT + tc) % 2
            w = [mmtgt]
            if sl in stage_free:
                w.append(stage_free[sl])
            stg = sb["qkstg"][:, sl * 512:(sl + 1) * 512]
            ev = op(C, "scalar", lambda e, stg=stg, bank=bank: e.activation(
                stg, bank[:, 0:512], AF.Identity), waits=w, inc=ss)
            stage_free[sl] = dma(C, "sync", f"dqk{sl}",
                                 lambda e, stg=stg, fi=fi, tc=tc: e.dma_start(
                                     out=C.dram["zT"][fi * 128:(fi + 1) * 128,
                                                      tc * 512:(tc + 1) * 512],
                                     in_=stg), waits=[(ss, ev)])
            return (ss, ev)
        return evac

    emit_proj_fm(C, C.aps["l3_wxz"], [i * 128 for i in range(8)], "xn16", ND,
                 xz_evac("x"), banks=(0, 1))
    emit_proj_fm(C, C.aps["l3_wxz"], [1024 + i * 128 for i in range(8)], "xn16", ND,
                 xz_evac("z"), banks=(0, 1))
    barrier(C)

    def dt_evac(fi, tc, bank, mmtgt):
        sl = tc % 2
        a = sb["sc32"][:, sl * 512:(sl + 1) * 512]
        w = [mmtgt]
        if C.l3dtf.get(sl):
            w.append(C.l3dtf[sl])
        v1 = op(C, "vector", lambda e, a=a, bank=bank, fi=fi: e.tensor_scalar(
            a, bank[:, 0:512], bias_col(C, "l3_bdt", fi), None, ALU.add),
            waits=w, inc=sv)
        sA = op(C, "scalar", lambda e, a=a: e.activation(a, a, AF.Abs),
                waits=[(sv, v1)], inc=ss)
        s1 = op(C, "scalar", lambda e, a=a: e.activation(a, a, AF.Exp, scale=-1.0),
                waits=[(ss, sA)], inc=ss)
        s2 = op(C, "scalar", lambda e, a=a: e.activation(a, a, AF.Ln, bias=1.0),
                waits=[(ss, s1)], inc=ss)
        v2 = op(C, "vector", lambda e, a=a, bank=bank, fi=fi: e.tensor_scalar(
            sb["st32"][:, 0:512], bank[:, 0:512], bias_col(C, "l3_bdt", fi), 0.0,
            ALU.add, ALU.max), waits=[(ss, s2)], inc=sv)
        v3 = op(C, "vector", lambda e, a=a, fi=fi, tc=tc: e.tensor_tensor(
            FM(C, "f32a", fi, tc * 512, 512), sb["st32"][:, 0:512], a, ALU.add),
            waits=[(sv, v2)], inc=sv)
        C.l3dtf[sl] = (sv, v3)
        return (sv, v3)

    C.l3dtf = {}
    emit_proj_fm(C, C.aps["l3_wdt"], [i * 128 for i in range(8)], "b16a", ND,
                 dt_evac, banks=(2, 3))
    for name, wname, rbase in [("B", "l3_wb", 0), ("Cc", "l3_wc", 32)]:
        tiles = stream_strip(C, C.aps[wname], 0, 16, 0, 1024)
        for tc in range(NT):
            bank = C.ps[4 + tc % 2]
            pw = [[tiles[kc][1]] if tc == 0 else [] for kc in range(ND)]
            g = mmgrp(C, bank[0:16, 0:512], [(tiles[kc][0], FM(C, "xn16", kc, tc * 512, 512))
                                             for kc in range(ND)],
                      per_mm_waits=pw, inc=st, bank_idx=4 + tc % 2)
            ev = op(C, "vector", lambda e, bank=bank, rbase=rbase, tc=tc: e.tensor_copy(
                sb["aux2"][rbase:rbase + 16, tc * 512:(tc + 1) * 512],
                bank[0:16, 0:512]), waits=[(st, g)], inc=sv)
            C.psfree[4 + tc % 2] = (sv, ev)
        for _, _, si in tiles:
            C.ring.consumed(si, st, st.n)
    barrier(C)

    def scan_phase(phase):
        for dc in range(ND):
            if phase == 1:
                ydma_prev = None
            for ci in range(NCH):
                lo = ci * CH
                uv = op(C, "vector", lambda e, dc=dc, lo=lo: e.tensor_tensor(
                    sb["st32"][:, 0:CH], FM(C, "f32a", dc, lo, CH),
                    FM(C, "b16a", dc, lo, CH), ALU.mult),
                    waits=[C.l3uf] if C.l3uf else [], inc=sv)
                if phase == 1:
                    ybank = C.ps[4 + ci % 2]
                    ygate = C.psfree.pop(4 + ci % 2, None)
                for n in range(16):
                    k = n * 8 + dc
                    dv = op(C, "scalar", lambda e, dc=dc, n=n, lo=lo: e.activation(
                        sb["scan32"][:, 0:CH], FM(C, "f32a", dc, lo, CH), AF.Exp,
                        scale=sb["acol"][:, dc * 16 + n:dc * 16 + n + 1]),
                        waits=[C.l3decf] if C.l3decf else [], inc=ss)
                    bb = C.ps[6 + n % 2]
                    bg = C.psfree.pop(6 + n % 2, None)
                    g = mmgrp(C, bb[:, 0:CH],
                              [(onehot_lhsT(C, n, 16), sb["aux2"][0:16, lo:lo + CH])],
                              waits=[bg] if bg else [], inc=st, bank_idx=None)
                    ip = op(C, "vector", lambda e, bb=bb: e.scalar_tensor_tensor(
                        sb["scan32"][:, CH:2 * CH], sb["st32"][:, 0:CH], 1.0,
                        bb[:, 0:CH], ALU.mult, ALU.mult),
                        waits=[(st, g), (sv, uv)], inc=sv)
                    C.psfree[6 + n % 2] = (sv, ip)
                    if phase == 0:
                        init = 0.0 if ci == 0 else sb["endst"][:, k:k + 1]
                    else:
                        init = sb["endst2"][:, k:k + 1]
                    scw = [(sv, ip), (ss, dv)]
                    if C.l3scf:
                        scw.append(C.l3scf)
                    sc = op(C, "vector", lambda e, init=init: e.tensor_tensor_scan(
                        sb["scan32"][:, 2 * CH:3 * CH], sb["scan32"][:, 0:CH],
                        sb["scan32"][:, CH:2 * CH], init, ALU.mult, ALU.add),
                        waits=scw, inc=sv)
                    C.l3decf = (sv, sc)
                    cw = op(C, "vector", lambda e, k=k, phase=phase: e.tensor_copy(
                        sb["endst" if phase == 0 else "endst2"][:, k:k + 1],
                        sb["scan32"][:, 3 * CH - 1:3 * CH]), waits=[(sv, sc)], inc=sv)
                    if phase == 0:
                        C.l3scf = (sv, cw)
                        continue
                    cb = C.ps[6 + n % 2]
                    cg = C.psfree.pop(6 + n % 2, None)
                    g2 = mmgrp(C, cb[:, 0:CH],
                               [(onehot_lhsT(C, n, 16, base=32), sb["aux2"][32:48, lo:lo + CH])],
                               waits=[cg] if cg else [], inc=st, bank_idx=None)
                    sC = op(C, "vector", lambda e, cb=cb, n=n: e.tensor_tensor(
                        sb["moe16"][:, (n % 2) * CH:(n % 2) * CH + CH],
                        sb["scan32"][:, 2 * CH:3 * CH], cb[:, 0:CH], ALU.mult),
                        waits=[(st, g2), (sv, cw), C.l3sCf.get(n % 2, (sv, 0))], inc=sv)
                    C.psfree[6 + n % 2] = (sv, sC)
                    C.l3scf = (sv, sC)
                    gy = mmgrp(C, ybank[:, 0:CH],
                               [(sb["idn16"][:, :], sb["moe16"][:, (n % 2) * CH:(n % 2) * CH + CH])],
                               waits=[(sv, sC)] + ([ygate] if (n == 0 and ygate) else []),
                               inc=st, start_first=(n == 0), stop_last=(n == 15))
                    C.l3sCf[n % 2] = (st, gy)
                if phase == 0:
                    continue
                zw = [ydma_prev] if ydma_prev else []
                zt = dma(C, "sync", "dz", lambda e, dc=dc, lo=lo: e.dma_start(
                    out=sb["qkstg"][:, 512:512 + CH],
                    in_=C.dram["zT"][dc * 128:(dc + 1) * 128, lo:lo + CH]), waits=zw)
                y2 = op(C, "vector", lambda e, dc=dc, lo=lo, ybank=ybank: e.scalar_tensor_tensor(
                    sb["st32"][:, 512:512 + CH], FM(C, "b16a", dc, lo, CH),
                    bias_col(C, "l3_dskip", dc), ybank[:, 0:CH], ALU.mult, ALU.add),
                    waits=[(st, st.n)], inc=sv)
                C.psfree[4 + ci % 2] = (sv, y2)
                sp = op(C, "scalar", lambda e, dc=dc: e.activation(
                    sb["st32"][:, 0:CH], sb["st32"][:, 512:512 + CH], AF.Sigmoid,
                    scale=10.0, bias=sb["beff"][:, dc:dc + 1]), waits=[(sv, y2)], inc=ss)
                ysp = op(C, "vector", lambda e: e.tensor_tensor(
                    sb["st32"][:, 512:512 + CH], sb["st32"][:, 512:512 + CH],
                    sb["st32"][:, 0:CH], ALU.mult), waits=[(ss, sp)], inc=sv)
                sz = op(C, "scalar", lambda e: e.activation(
                    sb["qkstg"][:, 512:512 + CH], sb["qkstg"][:, 512:512 + CH],
                    AF.Silu), waits=[zt, (sv, ysp)], inc=ss)
                yf = op(C, "vector", lambda e: e.tensor_tensor(
                    sb["qkstg"][:, 0:CH], sb["st32"][:, 512:512 + CH],
                    sb["qkstg"][:, 512:512 + CH], ALU.mult),
                    waits=[(ss, sz), C.l3yff] if C.l3yff else [(ss, sz)], inc=sv)
                C.l3yff = (sv, yf)
                ydma_prev = dma(C, "sync", "dy", lambda e, dc=dc, lo=lo: e.dma_start(
                    out=C.dram["yT"][dc * 128:(dc + 1) * 128, lo:lo + CH],
                    in_=sb["qkstg"][:, 0:CH]), waits=[(sv, yf)])
                C.l3uf = ydma_prev

    C.l3uf = None
    C.l3decf = None
    C.l3scf = None
    C.l3sCf = {}
    C.l3yff = None
    for dc in range(ND):
        v1 = op(C, "vector", lambda e, dc=dc: e.tensor_scalar(
            sb["beff"][:, dc:dc + 1], bias_col(C, "l3_vth", dc), 0.1, -10.0,
            ALU.max, ALU.mult), inc=sv)
    scan_phase(0)
    barrier(C)
    myst = dma(C, "gpsimd", "dsh", lambda e: e.dma_start(
        out=C.dram["shst"][bass.DynSlice(C.snap_half, 1), :].rearrange(
            "1 (p k) -> p k", p=128),
        in_=sb["endst"][:, :]))
    exch_flag_write(C, C.exch_base, [myst])
    rdy = exch_poll(C, C.exch_base)
    tps = dma(C, "gpsimd", "dpp", lambda e: e.dma_start(
        out=sb["endst2"][:, :],
        in_=C.dram["shst"][bass.DynSlice(C.snap_ohalf, 1), :].rearrange(
            "1 (p k) -> p k", p=128)), waits=[rdy])
    mk = op(C, "vector", lambda e: e.tensor_scalar(
        sb["endst2"][:, :], sb["endst2"][:, :], sb["m32c"][:, 0:1], 0.0,
        ALU.mult, ALU.add), waits=[tps], inc=sv)
    barrier(C)
    scan_phase(1)
    barrier(C)
    stage2 = {}

    def sout_evac(fi, tc, bank, mmtgt):
        sl = (fi * NT + tc) % 2
        w = [mmtgt]
        if sl in stage2:
            w.append(stage2[sl])
        ev = op(C, "vector", lambda e, bank=bank, sl=sl, fi=fi, tc=tc: e.tensor_tensor(
            sb["sc32"][:, sl * 512:(sl + 1) * 512], bank[:, 0:512],
            FM(C, "xn16", fi, tc * 512, 512), ALU.subtract), waits=w, inc=sv)
        stage2[sl] = dma(C, "sync", f"dso{sl}",
                         lambda e, sl=sl, fi=fi, tc=tc: e.dma_start(
                             out=C.dram["sT"][fi * 128:(fi + 1) * 128, tc * 512:(tc + 1) * 512],
                             in_=sb["sc32"][:, sl * 512:(sl + 1) * 512]),
                         waits=[(sv, ev)])
        return (sv, ev)

    stq = C.sem("stq")
    C.l3mmbase = stq.n + 1
    nload = [0]
    for fc in range(ND):
        tiles = stream_strip(C, C.aps["l3_wout"], fc * 128, 128, 0, 1024)
        for tc in range(NT):
            bank = C.ps[(fc * NT + tc) % 2]
            bgate = C.psfree.pop((fc * NT + tc) % 2, None)
            ldts = []
            for kc in range(ND):
                i = nload[0]
                nload[0] += 1
                sl = i % 4
                w = [(stq, C.l3mmbase + i - 3)] if i >= 4 else []
                ldts.append(dma(C, "sync", f"dys{sl}",
                                lambda e, kc=kc, tc=tc, sl=sl: e.dma_start(
                                    out=sb["kv16"][:, sl * 512:(sl + 1) * 512],
                                    in_=C.dram["yT"][kc * 128:(kc + 1) * 128,
                                                     tc * 512:(tc + 1) * 512]),
                                waits=w))

            @C.sec("tensor")
            def _(e, tiles=tiles, tc=tc, bank=bank, bgate=bgate, ldts=tuple(ldts)):
                for kc in range(ND):
                    s_, v_ = ldts[kc]
                    e.wait_ge(s_.h, v_)
                    if tc == 0:
                        e.wait_ge(tiles[kc][1][0].h, tiles[kc][1][1])
                    if kc == 0 and bgate is not None:
                        e.wait_ge(bgate[0].h, bgate[1])
                    inst = e.matmul(bank[:, 0:512], tiles[kc][0],
                                    sb["kv16"][:, (kc % 4) * 512:(kc % 4) * 512 + 512],
                                    start=(kc == 0), stop=(kc == ND - 1))
                    inst.then_inc(stq.h, 1)
                C.last["tensor"] = inst
                C.last_is_dma["tensor"] = False
                C.last_update["tensor"] = (stq, stq.n + ND)
            stq.bump(ND)
            C.psfree[(fc * NT + tc) % 2] = sout_evac(fc, tc, bank, (stq, stq.n))
        for _, _, si in tiles:
            C.ring.consumed(si, stq, stq.n)
    barrier(C)


def emit_l3_ffn(C):
    T, NT = C.T, C.NT
    sb = C.sb
    st, ss, sv = C.sem("s_t"), C.sem("s_s"), C.sem("s_v")
    for dc in range(ND):
        dma(C, "sync", "dxn", lambda e, dc=dc: e.dma_start(
            out=C.dram["xnT"][dc * 128:(dc + 1) * 128, :], in_=FM(C, "xn16", dc, 0, T)))
    barrier(C)
    rows_done = emit_stats_rows(C, "xn16", is16=True)
    emit_ln_apply(C, "xn16", rows_done, dst="xn16", gamma="l3_lng", beta="l3_lnb")
    barrier(C)
    ab_dma = {}
    for tc in range(NT):
        for half in range(2):
            for hh in range(16):
                hc = half * 16 + hh
                tiles = stream_strip(C, C.aps["l3_aw1"], hc * 128, 128, 0, 1024)
                b = hh % 2
                pw = [[tiles[kc][1]] for kc in range(ND)]
                g = mmgrp(C, C.ps[b][:, 0:512],
                          [(tiles[kc][0], FM(C, "xn16", kc, tc * 512, 512))
                           for kc in range(ND)],
                          per_mm_waits=pw, inc=st, bank_idx=b)
                ev = op(C, "scalar", lambda e, b=b, hh=hh, hc=hc: e.activation(
                    sb["b16a"][:, hh * 512:(hh + 1) * 512], C.ps[b][:, 0:512], AF.Gelu,
                    bias=bias_col(C, "l3_ab1", hc)), waits=[(st, g)], inc=ss)
                C.psfree[b] = (ss, ev)
                for _, _, si in tiles:
                    C.ring.consumed(si, st, st.n)
            gelu_done = (ss, ss.n)
            for dc in range(ND):
                tiles = stream_strip(C, C.aps["l3_aw2"], dc * 128, 128,
                                     half * 2048, 2048)
                b = 2 + dc % 2
                pw = []
                for hh in range(16):
                    w = [gelu_done] if hh == 0 else []
                    w.append(tiles[hh][1])
                    pw.append(w)
                g = mmgrp(C, C.ps[b][:, 0:512],
                          [(tiles[hh][0], sb["b16a"][:, hh * 512:(hh + 1) * 512])
                           for hh in range(16)],
                          per_mm_waits=pw, inc=st, bank_idx=b)
                if half == 0:
                    ev = op(C, "vector", lambda e, b=b, dc=dc, tc=tc: e.tensor_copy(
                        FM(C, "f32a", dc, tc * 512, 512), C.ps[b][:, 0:512]),
                        waits=[(st, g)], inc=sv)
                    C.psfree[b] = (sv, ev)
                else:
                    sl = dc % 2
                    w2 = [(st, g)]
                    if sl in ab_dma:
                        w2.append(ab_dma[sl])
                    av = op(C, "vector", lambda e, b=b, dc=dc, tc=tc: e.tensor_tensor(
                        sb["st32"][:, (dc % 2) * 512:(dc % 2) * 512 + 512],
                        FM(C, "f32a", dc, tc * 512, 512), C.ps[b][:, 0:512], ALU.add),
                        waits=w2, inc=sv)
                    C.psfree[b] = (sv, av)
                    ev = op(C, "scalar", lambda e, sl=sl, dc=dc: e.activation(
                        sb["qkstg"][:, sl * 512:(sl + 1) * 512],
                        sb["st32"][:, sl * 512:(sl + 1) * 512],
                        AF.Identity, bias=bias_col(C, "l3_ab2", dc)),
                        waits=[(sv, av)], inc=ss)
                    ab_dma[sl] = dma(C, "sync", f"dab{sl}",
                                     lambda e, sl=sl, dc=dc, tc=tc: e.dma_start(
                                         out=C.dram["abelT"][dc * 128:(dc + 1) * 128,
                                                             tc * 512:(tc + 1) * 512],
                                         in_=sb["qkstg"][:, sl * 512:(sl + 1) * 512]),
                                     waits=[(ss, ev)])
                for _, _, si in tiles:
                    C.ring.consumed(si, st, st.n)
    barrier(C)
    emit_gating(C, 6, 2, None, "softmax", pre=True)
    b2t = dma(C, "sync", "db2", lambda e: e.dma_start(
        out=sb["sc32"][0:6, 0:1024], in_=C.aps["l3_eb2m"][:, :]))
    cvt = op(C, "vector", lambda e: e.tensor_copy(
        sb["moe16"][0:6, 0:1024], sb["sc32"][0:6, 0:1024]), waits=[b2t], inc=sv)
    wcv = op(C, "vector", lambda e: e.tensor_copy(
        sb["moe16"][0:16, 1024:1024 + T], sb["rows64"][0:16, T:2 * T]),
        waits=[(sv, sv.n)], inc=sv)
    for dc in range(ND):
        for tc in range(NT):
            b = 4 + (dc * NT + tc) % 2
            g = mmgrp(C, C.ps[b][:, 0:512],
                      [(sb["moe16"][0:6, dc * 128:(dc + 1) * 128],
                        sb["moe16"][0:6, 1024 + tc * 512:1024 + (tc + 1) * 512])],
                      waits=[(sv, max(cvt, wcv))], inc=st, bank_idx=b)
            a1 = op(C, "vector", lambda e, dc=dc, tc=tc, b=b: e.tensor_copy(
                FM(C, "f32a", dc, tc * 512, 512), C.ps[b][:, 0:512]),
                waits=[(st, g)], inc=sv)
            C.psfree[b] = (sv, a1)
    HC = 16
    for e_idx in range(6):
        for tc in range(NT):
            wrep, wg = bcast_gate(C, e_idx, tc, 6)
            cw = op(C, "scalar", lambda e, wrep=wrep: e.activation(
                sb["gw16"][:, 0:512], wrep[:, 0:512], AF.Identity),
                waits=[wg], inc=ss)
            C.psfree[6] = (ss, cw)
            for hc in range(HC):
                t1 = stream_strip(C, C.aps["l3_ew1"][e_idx], hc * 128, 128, 0, 1024)
                b = hc % 2
                pw = [[t1[kc][1]] for kc in range(ND)]
                g = mmgrp(C, C.ps[b][:, 0:512],
                          [(t1[kc][0], FM(C, "xn16", kc, tc * 512, 512))
                           for kc in range(ND)],
                          per_mm_waits=pw, inc=st, bank_idx=b)
                gl = op(C, "scalar", lambda e, b=b, e_idx=e_idx, hc=hc: e.activation(
                    sb["st32"][:, (hc % 2) * 512:(hc % 2) * 512 + 512], C.ps[b][:, 0:512],
                    AF.Gelu, bias=bias_col(C, "l3_eb1", e_idx * 16 + hc)),
                    waits=[(st, g), C.l3glf.get(hc % 2, (ss, 0))], inc=ss)
                C.psfree[b] = (ss, gl)
                gdst = (sb["moe16"][:, hc * 512:(hc + 1) * 512] if hc < 11 else
                        sb["b16a"][:, 4096 + (hc - 11) * 512:4096 + (hc - 10) * 512])
                gm = op(C, "vector", lambda e, gdst=gdst, hc=hc: e.tensor_tensor(
                    gdst, sb["st32"][:, (hc % 2) * 512:(hc % 2) * 512 + 512],
                    sb["gw16"][:, 0:512], ALU.mult), waits=[(ss, gl), (ss, cw)], inc=sv)
                C.l3glf[hc % 2] = (sv, gm)
                for _, _, si in t1:
                    C.ring.consumed(si, st, st.n)
            g_done = (sv, sv.n)
            for dc in range(ND):
                t3 = stream_strip(C, C.aps["l3_ew2"][e_idx], dc * 128, 128, 0, HC * 128)
                b = 4 + dc % 2
                pw = []
                for hc in range(HC):
                    w = [g_done] if hc == 0 else []
                    w.append(t3[hc][1])
                    pw.append(w)
                g = mmgrp(C, C.ps[b][:, 0:512],
                          [(t3[hc][0],
                            sb["moe16"][:, hc * 512:(hc + 1) * 512] if hc < 11 else
                            sb["b16a"][:, 4096 + (hc - 11) * 512:4096 + (hc - 10) * 512])
                           for hc in range(HC)],
                          per_mm_waits=pw, inc=st, bank_idx=b)
                a1 = op(C, "vector", lambda e, dc=dc, tc=tc, b=b: e.tensor_tensor(
                    FM(C, "f32a", dc, tc * 512, 512), FM(C, "f32a", dc, tc * 512, 512),
                    C.ps[b][:, 0:512], ALU.add), waits=[(st, g)], inc=sv)
                C.psfree[b] = (sv, a1)
                for _, _, si in t3:
                    C.ring.consumed(si, st, st.n)
    barrier(C)
    if getattr(C, "dbg_moe", None) is not None:
        for dc in range(ND):
            dma(C, "sync", "dbgm", lambda e, dc=dc: e.dma_start(
                out=C.dbg_moe[:, dc * T:(dc + 1) * T], in_=FM(C, "f32a", dc, 0, T)))
        barrier(C)
    for rnd in range(4):
        for dc in range(ND):
            for tc in range(NT):
                cv = op(C, "vector", lambda e, dc=dc, tc=tc: e.tensor_copy(
                    sb["b16a"][:, (dc * NT + tc) * 512:(dc * NT + tc) * 512 + 512],
                    FM(C, "f32a", dc, tc * 512, 512)), inc=sv)
        cast_done = (sv, sv.n)
        barrier(C)
        for fc in range(ND):
            tiles = stream_strip(C, C.dram[f"rot{rnd}"], fc * 128, 128, 0, 1024)
            for tc in range(NT):
                b = (fc * NT + tc) % 2
                pw = []
                for kc in range(ND):
                    w = [cast_done] if kc == 0 else []
                    if tc == 0:
                        w.append(tiles[kc][1])
                    pw.append(w)
                g = mmgrp(C, C.ps[b][:, 0:512],
                          [(tiles[kc][0],
                            sb["b16a"][:, (kc * NT + tc) * 512:(kc * NT + tc) * 512 + 512])
                           for kc in range(ND)],
                          per_mm_waits=pw, inc=st, bank_idx=b)
                ev = op(C, "vector", lambda e, b=b, fc=fc, tc=tc: e.tensor_copy(
                    FM(C, "f32a", fc, tc * 512, 512), C.ps[b][:, 0:512]),
                    waits=[(st, g)], inc=sv)
                C.psfree[b] = (sv, ev)
            for _, _, si in tiles:
                C.ring.consumed(si, st, st.n)
        barrier(C)
        if getattr(C, "dbg_rot", None) is not None:
            for dc in range(ND):
                dma(C, "sync", "dbgr", lambda e, dc=dc, rnd=rnd: e.dma_start(
                    out=C.dbg_rot[rnd][:, dc * T:(dc + 1) * T],
                    in_=FM(C, "f32a", dc, 0, T)))
            barrier(C)
    for dc in range(ND):
        for tc in range(NT):
            op(C, "vector", lambda e, dc=dc, tc=tc: e.tensor_copy(
                sb["b16a"][:, (dc * NT + tc) * 512:(dc * NT + tc) * 512 + 512],
                FM(C, "f32a", dc, tc * 512, 512)), inc=sv)
    nb_done = (sv, sv.n)
    off_gcw, _ = C.bias_reg["l3_gcw"]
    gc0 = op(C, "vector", lambda e, off_gcw=off_gcw: e.tensor_copy(
        sb["sc16"][:, 0:16], sb["bias32"][:, off_gcw:off_gcw + 16]), inc=sv)
    gcw = [(sb["sc16"][:, kc:kc + 1], (sv, gc0)) for kc in range(16)]
    abg = {}
    for tc in range(NT):
        b = 4 + tc % 2
        pairs, pw = [], []
        for kc in range(4):
            w = [abg[kc]] if kc in abg else []
            t = dma(C, "sync", f"dys{kc}", lambda e, kc=kc, tc=tc: e.dma_start(
                out=sb["kv16"][:, kc * 512:(kc + 1) * 512],
                in_=C.dram["abelT"][kc * 128:(kc + 1) * 128, tc * 512:(tc + 1) * 512]),
                waits=w)
            pairs.append((gcw[kc][0], sb["kv16"][:, kc * 512:(kc + 1) * 512]))
            pw.append([t, gcw[kc][1], nb_done])
        g1 = mmgrp(C, C.ps[b][0:1, 0:512], pairs, per_mm_waits=pw, inc=st,
                   bank_idx=b, stop_last=False)
        pairs, pw = [], []
        for kc in range(4, ND):
            sl = kc % 4
            t = dma(C, "sync", f"dys{sl}", lambda e, kc=kc, tc=tc, sl=sl: e.dma_start(
                out=sb["kv16"][:, sl * 512:(sl + 1) * 512],
                in_=C.dram["abelT"][kc * 128:(kc + 1) * 128, tc * 512:(tc + 1) * 512]),
                waits=[(st, g1)])
            pairs.append((gcw[kc][0], sb["kv16"][:, sl * 512:(sl + 1) * 512]))
            pw.append([t, gcw[kc][1]])
        for kc in range(ND):
            pairs.append((gcw[8 + kc][0],
                          sb["b16a"][:, (kc * NT + tc) * 512:(kc * NT + tc) * 512 + 512]))
            pw.append([gcw[8 + kc][1]])
        g = mmgrp(C, C.ps[b][0:1, 0:512], pairs, per_mm_waits=pw, inc=st,
                  start_first=False, bank_idx=None)
        for s4 in range(4):
            abg[s4] = (st, g)
        gt = op(C, "scalar", lambda e, b=b, tc=tc: e.activation(
            sb["rows64"][32:33, tc * 512:(tc + 1) * 512], C.ps[b][0:1, 0:512],
            AF.Sigmoid, bias=bias_col(C, "l3_gcb")[0:1, :]), waits=[(st, g)], inc=ss)
        C.psfree[b] = (ss, gt)
    gate_done = (ss, ss.n)
    abg2 = {}
    for tc in range(NT):
        b = 4 + tc % 2
        gb = mmgrp(C, C.ps[b][:, 0:512],
                   [(ONESC(C, 1, 128, base=32),
                     sb["rows64"][32:33, tc * 512:(tc + 1) * 512])],
                   waits=[gate_done] + ([C.psfree.pop(b)] if b in C.psfree else []),
                   inc=st)
        gr = op(C, "vector", lambda e, b=b: e.tensor_copy(
            sb["st32"][:, 0:512], C.ps[b][:, 0:512]), waits=[(st, gb)], inc=sv)
        C.psfree[b] = (sv, gr)
        for dc in range(ND):
            sl = dc % 4
            w = [abg2[sl]] if sl in abg2 else []
            t = dma(C, "sync", f"dys{sl}", lambda e, dc=dc, tc=tc, sl=sl: e.dma_start(
                out=sb["kv16"][:, sl * 512:(sl + 1) * 512],
                in_=C.dram["abelT"][dc * 128:(dc + 1) * 128, tc * 512:(tc + 1) * 512]),
                waits=w)
            d1 = op(C, "vector", lambda e, dc=dc, tc=tc, sl=sl: e.tensor_tensor(
                sb["st32"][:, 512:1024], FM(C, "f32a", dc, tc * 512, 512),
                sb["kv16"][:, sl * 512:(sl + 1) * 512], ALU.subtract),
                waits=[t, (sv, gr)], inc=sv)
            d2 = op(C, "vector", lambda e: e.tensor_tensor(
                sb["st32"][:, 512:1024], sb["st32"][:, 512:1024],
                sb["st32"][:, 0:512], ALU.mult), waits=[(sv, d1)], inc=sv)
            d3 = op(C, "vector", lambda e, dc=dc, tc=tc, sl=sl: e.tensor_tensor(
                sb["st32"][:, 512:1024], sb["st32"][:, 512:1024],
                sb["kv16"][:, sl * 512:(sl + 1) * 512], ALU.add), waits=[(sv, d2)], inc=sv)
            tx = dma(C, "sync", "dxn2", lambda e, dc=dc, tc=tc: e.dma_start(
                out=sb["kv16"][:, 2048:2560],
                in_=C.dram["xnT"][dc * 128:(dc + 1) * 128, tc * 512:(tc + 1) * 512]),
                waits=[abg2.get(9, (sv, 0))] if 9 in abg2 else [])
            d4 = op(C, "vector", lambda e, dc=dc, tc=tc: e.tensor_tensor(
                FM(C, "f32a", dc, tc * 512, 512), sb["st32"][:, 512:1024],
                sb["kv16"][:, 2048:2560], ALU.add), waits=[(sv, d3), tx], inc=sv)
            abg2[sl] = (sv, d4)
            abg2[9] = (sv, d4)
    barrier(C)
    if getattr(C, "dbg_gate", None) is not None:
        dma(C, "sync", "dbgg", lambda e: e.dma_start(
            out=C.dbg_gate[0:1, :], in_=sb["rows64"][32:33, 0:T]))
        dma(C, "sync", "dbgw", lambda e: e.dma_start(
            out=C.dbg_wrows[:, :], in_=sb["rows64"][0:16, T:2 * T]))
        barrier(C)


def emit_layer3(C, x_in, x_out):
    C.l3glf = {}
    dma(C, "gpsimd", "didn", lambda e: e.dma_start(
        out=C.sb["idn16"][:, :], in_=C.aps["identity"][:, :]))
    dma(C, "sync", "dmc", lambda e: e.dma_start(
        out=C.sb["m32c"][:, 0:1], in_=C.aps["oddcol"][:, :]))
    emit_l3_expm(C)
    emit_ln_primary(C, x_in)
    emit_l3_gate_logits(C)
    emit_l3_ssm(C)
    emit_l3_ffn(C)
    emit_combine(C, x_in, x_out)



import contextlib

LAUNCH_GROUPS = [[0], [1], [2], [3]]
LAYER_EMITTERS = None


def build_program(layers, T, bias_reg, nbias):
    nc = bass.Bass("TRN2", target_bir_lowering=False,
                   detect_race_conditions=False)
    nc._stack = contextlib.ExitStack()
    innames = set(COMMON_INPUTS)
    for L in layers:
        innames.update(LAYER_INPUTS[L])
    innames.discard("xT")
    aps = {}
    for name in sorted(innames):
        shp = INPUT_SHAPES[name]
        dt = U32 if name in ("nonce", "slot") else F32
        aps[name] = nc.dram_tensor(name, list(shp), dt, kind="ExternalInput").ap()
    aps["xT"] = nc.dram_tensor("xT", [1024, T], F32, kind="ExternalInput").ap()
    x_out = nc.dram_tensor("xO", [1024, T], F32, kind="ExternalOutput").ap()
    xmid = [nc.dram_tensor(f"xmid{i}", [1024, T], F32, kind="Internal").ap()
            for i in range(2)] if len(layers) > 1 else []
    stack = contextlib.ExitStack()
    with stack:
        with nc.Block() as block:
            C = Ctx(nc, block, T)
            C.bias_reg = bias_reg
            C.nbias = nbias
            C.aps = aps
            alloc_buffers(C, stack)
            declare_common_dram(C)
            load_constants(C, aps)
            emitters = {0: emit_layer0, 1: emit_layer1, 2: emit_layer2, 3: emit_layer3}
            for i, L in enumerate(layers):
                declare_layer_dram(C, L)
                C.exch_base = L
                x_in = aps["xT"] if i == 0 else xmid[(i + 1) % 2]
                x_o = x_out if i == len(layers) - 1 else xmid[i % 2]
                emitters[L](C, x_in, x_o)
        nc._stack.close()
    return nc, innames


INPUT_SHAPES = None


def _compute_input_shapes(per_core0):
    global INPUT_SHAPES
    INPUT_SHAPES = {k: v.shape for k, v in per_core0.items()}


_PROG_CACHE = {}


def kernel(**inputs):
    T = 2048
    per_core, bias_reg = host_prepare(inputs, T)
    _compute_input_shapes(per_core[0])
    nbias = per_core[0]["biases"].shape[1]
    xcur = [m["xT"] for m in per_core]
    for group in LAUNCH_GROUPS:
        key = tuple(group)
        if key not in _PROG_CACHE:
            _PROG_CACHE[key] = build_program(group, T, bias_reg, nbias)
        nc, innames = _PROG_CACHE[key]
        in_maps = []
        for c in range(8):
            m = {k: per_core[c][k] for k in innames if k != "xT"}
            m["xT"] = np.ascontiguousarray(xcur[c], np.float32)
            in_maps.append(m)
        res = run_bass_kernel_spmd(nc, in_maps, core_ids=list(range(8)))
        xcur = [res.results[c]["xO"] for c in range(8)]
    out = np.empty((4, 4096, 1024), np.float32)
    for c in range(8):
        out[c // 2, (c % 2) * T:(c % 2) * T + T, :] = np.asarray(xcur[c]).T
    return out
